# version 12
# speedup vs baseline: 2.1178x; 1.0073x over previous
import numpy as np
import ml_dtypes
from contextlib import ExitStack

import concourse.bass as bass
import concourse.mybir as mybir
from concourse.bass_utils import run_bass_kernel_spmd
from concourse.aot_env import allow_ici_all_to_all

F32 = mybir.dt.float32
BF16 = mybir.dt.bfloat16
AF = mybir.ActivationFunctionType
OP = mybir.AluOpType
AX = mybir.AxisListType

P = 128
L = 8192
D = 1024
DQ = 256
NST = 16
E = 96
RK = 64
NBLK = 4
TB = 2048
LT = TB // P
NCH = 4
CH = 512
EPS = 1e-5

ENGS = ("sync", "vector", "scalar", "tensor", "gpsimd")


class Plan:
    def __init__(self):
        self.items = {e: [] for e in ENGS}
        self.cnt = {}
        self.marks = {}
        self.last = {e: None for e in ENGS}

    def sem(self, name):
        if name not in self.cnt:
            self.cnt[name] = 0

    def stage(self, eng, fn, waits=(), inc=None, dma=False, tag=None):
        for w in waits:
            assert w in self.marks, f"stage tag={tag}: unknown wait {w}"
        self.gidx = getattr(self, "gidx", 0) + 1
        self.ser_before = getattr(self, "ser_after", 0)
        amt = 16 * getattr(fn, "dma_n", 1) if dma else 1
        self.ser_after = self.ser_before + amt
        self.items[eng].append((list(waits), fn, inc, dma, self.gidx,
                                dict(self.cnt), amt))
        if inc is not None:
            self.sem(inc)
            if dma:
                self.cnt[inc] += 16 * getattr(fn, "dma_n", 1)
            else:
                self.cnt[inc] += 1
        if tag is not None:
            assert inc is not None, tag
            self.marks[tag] = (inc, self.cnt[inc])
            self.last[eng] = tag

    def alias(self, tag, src):
        self.marks[tag] = self.marks[src]

FENCE = {"cnt": {}, "sems": {}, "eng": None}

def fence(stream, inst):
    e = FENCE["eng"]
    FENCE["cnt"][e] += 1
    inst.then_inc(FENCE["sems"][e], 1)
    stream.wait_ge(FENCE["sems"][e], FENCE["cnt"][e])


def dma_n(n):
    def deco(fn):
        fn.dma_n = n
        return fn
    return deco


def build_program(nblk=NBLK, max_stages=None, dbg=False, serialize=False):
    global NBLK, L
    NBLK = nblk
    L = NBLK * TB
    nc = bass.Bass("TRN2", target_bir_lowering=False, debug=False, detect_race_conditions=False)

    def din(name, shape, dt=F32):
        return nc.dram_tensor(name, shape, dt, kind="ExternalInput").ap()

    u_b = din("u_b", [L, D])
    u_q = din("u_q", [L, DQ])
    u_tok = din("u_tok", [NBLK, CH, D])
    wxt = din("wxt", [D, E])
    wdtq = din("wdtq", [RK, DQ])
    aq = din("aq", [P, 32])
    bdtbc = din("bdtbc", [P, DQ])
    dbc_in = din("dbc", [P, DQ])
    masks_in = din("masks", [P, 2])
    ident_in = din("ident_in", [P, P])
    cn_w1 = din("cn_w1", [P, 8, 8, P])
    cn_w2 = din("cn_w2", [P, 8, 4, P])
    cn_w3 = din("cn_w3", [P, 4, 2, P])
    cn_w4 = din("cn_w4", [P, 2])
    cn_b1 = din("cn_b1", [P, 8])
    cn_b2 = din("cn_b2", [P, 4])
    cn_b3 = din("cn_b3", [P, 2])
    cn_b4 = din("cn_b4", [1, 1])
    gc_w1 = din("gc_w1", [P, 8, 4, P])
    gc_w2 = din("gc_w2", [P, 4, 2, P])
    gc_w3 = din("gc_w3", [P, 2])
    gc_b1 = din("gc_b1", [P, 4])
    gc_b2 = din("gc_b2", [P, 2])
    gc_b3 = din("gc_b3", [1, 1])
    arr_in = din("arr", [1, 1])

    out_tok = nc.dram_tensor("out_tok", [NBLK, CH, D], F32, kind="ExternalOutput").ap()

    dk = dict(kind="ExternalOutput") if dbg else {}
    a2a_in = [nc.dram_tensor(f"a2a_in{k}", [8 * CH, DQ], F32) for k in range(NBLK)]
    a2a_out = [nc.dram_tensor(f"a2a_out{k}", [8 * CH, DQ], F32) for k in range(NBLK)]
    gfar_in = nc.dram_tensor("gfar_in", [1, D], F32)
    gfar_out = nc.dram_tensor("gfar_out", [1, D], F32)
    d_attn = nc.dram_tensor("d_attn", [NBLK, TB, DQ], F32, **dk) if dbg else None
    d_gm = nc.dram_tensor("d_gm", [2, D], F32, **dk) if dbg else None
    d_xdbl = nc.dram_tensor("d_xdbl", [NBLK, TB, 32], F32, **dk) if dbg else None
    d_dt = nc.dram_tensor("d_dt", [NBLK, TB, DQ], F32, **dk) if dbg else None
    d_dtu = nc.dram_tensor("d_dtu", [NBLK, TB, DQ], BF16, **dk) if dbg else None
    d_ln1d = nc.dram_tensor("d_ln1d", [NBLK, TB, DQ], F32, **dk) if dbg else None
    d_stats = nc.dram_tensor("d_stats", [NBLK, P, 48], F32, **dk) if dbg else None
    d_ln1e = nc.dram_tensor("d_ln1e", [NBLK, TB, DQ], F32, **dk) if dbg else None
    d_ccast = nc.dram_tensor("d_ccast", [NBLK, TB, 2 * NST], BF16, **dk) if dbg else None
    x2_sp = nc.dram_tensor("x2_sp", [NBLK, CH, D], F32, **dk)
    ln2_sp = nc.dram_tensor("ln2_sp", [NBLK, CH, D], F32, **dk)
    tf0_sp = nc.dram_tensor("tf0_sp", [NBLK, CH], F32, **dk)
    crows_hbm = nc.dram_tensor("crows_hbm", [NBLK, 2 * NST, TB], BF16, **dk)

    ctx = ExitStack()

    def sbt(name, shape, dt=F32):
        return ctx.enter_context(nc.sbuf_tensor(name, shape, dt))

    def pst(name, shape, dt=F32):
        return ctx.enter_context(nc.psum_tensor(name, shape, dt))

    wxt_b = sbt("wxt_b", [P, 8, E], BF16)
    wdt_b = sbt("wdt_b", [RK, DQ], BF16)
    acols = sbt("acols", [P, 32])
    bdtbc_s = sbt("bdtbc_s", [P, DQ])
    dbc_s = sbt("dbc_s", [P, DQ])
    masks = sbt("masks_s", [P, 2])
    w1b = sbt("w1b", [P, 8, 8, P], BF16)
    w2b = sbt("w2b", [P, 8, 4, P], BF16)
    w3b = sbt("w3b", [P, 4, 2, P], BF16)
    w4b = sbt("w4b", [P, 2], BF16)
    b1c = sbt("b1c", [P, 8])
    b2c = sbt("b2c", [P, 4])
    b3c = sbt("b3c", [P, 2])
    bias4 = sbt("bias4", [1, 2])
    gb1c = sbt("gb1c", [P, 4])
    gb2c = sbt("gb2c", [P, 2])
    gw3b = sbt("gw3b", [P, 2], BF16)
    identb = sbt("identb", [P, P], BF16)
    identf = sbt("identf", [P, P])
    ones_col_b = sbt("ones_col_b", [P, 1], BF16)
    ones_row_b = sbt("ones_row_b", [1, P], BF16)
    ones_row_f = sbt("ones_row_f", [1, P])
    consts = sbt("consts", [P, 8])
    statec = sbt("statec", [P, 32])
    wsum96 = sbt("wsum96", [P, E])
    wvbc = sbt("wvbc", [P, DQ])
    mcol = sbt("mcol", [P, LT])
    rcol = sbt("rcol", [P, LT])
    nmrcol = sbt("nmrcol", [P, LT])
    rowsc = sbt("rowsc", [1, 8, P])
    rowsb = sbt("rowsb", [1, 8, P], BF16)
    colsc = sbt("colsc", [P, 8])
    colsb = sbt("colsb", [P, 2], BF16)

    uq_t = sbt("uq_t", [P, 2, DQ])
    ln1d = sbt("ln1d", [P, LT, DQ])
    ln1s = sbt("ln1s", [P, 2, DQ])
    spbuf = sbt("spbuf", [P, 2, DQ])
    tmbuf = sbt("tmbuf", [P, LT, DQ])
    dt_fm = sbt("dt_fm", [P, 2, TB])
    xdbl_tm = sbt("xdbl_tm", [P, LT, 32])
    xdbl_fm = sbt("xdbl_fm", [E, TB])
    dtraw_fm = sbt("dtraw_fm", [RK, TB], BF16)
    u_tm = sbt("u_tm", [P, 2, D])
    u_bfq = sbt("u_bfq", [P, 4, D], BF16)
    xbuf = sbt("xbuf", [P, 4, 8, P], BF16)
    dtu = sbt("dtu", [P, LT, DQ], BF16)
    dbsl = sbt("dbsl", [P, 2, 8, DQ], BF16)
    absl = sbt("absl", [P, 2, 1024])
    hsl = sbt("hsl", [P, 1024], BF16)
    tsl = sbt("tsl", [P, 2, 1024], BF16)
    crows = sbt("crows", [2 * NST, LT, P], BF16)
    ccast = sbt("ccast", [P, LT, 2 * NST], BF16)
    cbc = sbt("cbc", [P, 2, 1024], BF16)
    bbc = sbt("bbc", [P, 2, 1024], BF16)
    dbT = sbt("dbT", [P, 1024], BF16)
    ybuf = sbt("ybuf", [P, CH])

    fh1 = sbt("fh1", [P, 8, CH], BF16)
    frow = sbt("frow", [1, CH])
    gmr = sbt("gmr", [1, D])
    gmc = sbt("gmc", [P, 8], BF16)
    gfh1 = sbt("gfh1", [P, 4], BF16)
    gfh2 = sbt("gfh2", [P, 2], BF16)
    abcol = sbt("abcol", [P, 4])
    m2col = sbt("m2col", [P, 4])
    r2col = sbt("r2col", [P, 4])
    nm2col = sbt("nm2col", [P, 4])
    tfc = sbt("tfc", [P, NBLK, 4])
    ncc = sbt("ncc", [P, 4])
    ccol = sbt("ccol", [P, 4])

    fx2 = ln1d[:, :, :].rearrange("p a b -> p (a b)").rearrange("p (a b) -> p a b", b=D)
    fln2 = tmbuf[:, :, :].rearrange("p a b -> p (a b)").rearrange("p (a b) -> p a b", b=D)
    fob = dt_fm[:, :, :].rearrange("p a b -> p (a b)").rearrange("p (a b) -> p a b", b=D)
    fbf = absl[:, :, :].rearrange("p a b -> p (a b)").bitcast(BF16).rearrange("p (a b) -> p a b", b=D)
    ffm = dtu[:, :, :].rearrange("p a b -> p (a b)").rearrange("p (a b c) -> p a b c", b=8, c=P)
    dtuT = dbsl[:, :, :, :].rearrange("p a b c -> p (a b c)").rearrange("p (a b c) -> p a b c", b=2, c=P)
    fh2 = dbsl[:, :, :, :].rearrange("p a b c -> p (a b c)").rearrange("p (a b) -> p a b", b=CH)[:, 0:4, :]
    fh3 = hsl[:, :].rearrange("p (a b) -> p a b", b=CH)
    gw1b = u_bfq[:, :, :].rearrange("p a b -> p (a b)").rearrange("p (a b c) -> p a b c", b=4, c=P)
    gw2b = tsl[:, :, :].rearrange("p a b -> p (a b)").rearrange("p (a b c) -> p a b c", b=2, c=P)[:, 0:4, :, :]
    ytmp = ybuf[:, :].rearrange("p (a b) -> p a b", b=DQ)

    py = [pst(f"py{i}", [P, CH]) for i in range(4)]
    pdb0 = pst("pdb0", [P, 1024], BF16)
    pdb1 = pst("pdb1", [P, 1024], BF16)
    pst0 = pst("pst0", [P, CH])
    pst1 = pst("pst1", [P, CH])

    pl = Plan()

    cast_loads = [
        (wxt_b[:, :, :], wxt.rearrange("(a p) e -> p a e", p=P)),
        (wdt_b[:, :], wdtq),
        (w1b[:, :, :, :], cn_w1), (w2b[:, :, :, :], cn_w2),
        (w3b[:, :, :, :], cn_w3), (w4b[:, :], cn_w4),
        (gw3b[:, :], gc_w3),
    ]

    def g_cast_loads(g):
        ins = []
        for dst, src in cast_loads:
            ins.append(g.dma_start(out=dst, in_=src))
        return ins
    g_cast_loads.dma_n = len(cast_loads)
    pl.stage("gpsimd", g_cast_loads, inc="dG", dma=True, tag="wl_cast")

    plain_loads = [
        (bdtbc_s[:, :], bdtbc), (dbc_s[:, :], dbc_in), (masks[:, :], masks_in),
        (b1c[:, :], cn_b1), (b2c[:, :], cn_b2), (b3c[:, :], cn_b3),
        (gb1c[:, :], gc_b1), (gb2c[:, :], gc_b2),
        (identf[:, :], ident_in), (acols[:, :], aq),
        (bias4[0:1, 0:1], cn_b4), (bias4[0:1, 1:2], gc_b3),
        (abcol[0:1, 3:4], arr_in),
    ]

    def s_plain_loads(s):
        ins = []
        for dst, src in plain_loads:
            ins.append(s.dma_start(out=dst, in_=src))
        return ins
    s_plain_loads.dma_n = len(plain_loads)
    pl.stage("sync", s_plain_loads, inc="dWL", dma=True, tag="wl_plain")

    def v_setup(v):
        v.memset(consts[:, 0:1], EPS)
        v.memset(consts[:, 1:2], 1.0)
        v.memset(consts[:, 2:3], -1.0)
        v.memset(consts[:, 3:4], -0.5)
        v.memset(consts[:, 4:5], 0.5)
        v.memset(statec[:, :], 0.0)
        v.memset(ones_col_b[:, :], 1.0)
        v.memset(ones_row_b[0:1, :], 1.0)
        v.memset(ones_row_f[0:1, :], 1.0)
        return v.tensor_copy(out=identb[:, :], in_=identf[:, :])
    pl.stage("vector", v_setup, waits=["wl_plain"], inc="sV", tag="v_setup")

    def v_halfb4(v):
        return v.tensor_scalar(out=bias4[0:1, 0:2], in0=bias4[0:1, 0:2],
                               scalar1=0.5, scalar2=None, op0=OP.mult)
    pl.stage("vector", v_halfb4, waits=["wl_plain"], inc="sV", tag="v_halfb4")

    def s_acols(s):
        return s.activation(out=acols[:, :], in_=acols[:, :], func=AF.Exp)
    pl.stage("scalar", s_acols, waits=["wl_plain"], inc="sS", tag="acols_exp")

    def v_acols(v):
        return v.tensor_scalar(out=acols[:, :], in0=acols[:, :], scalar1=-1.0,
                               scalar2=None, op0=OP.mult)
    pl.stage("vector", v_acols, waits=["acols_exp"], inc="sV", tag="acols_neg")

    def t_wsum(t):
        ins = None
        for kb in range(8):
            ins = t.matmul(pst0[0:1, 0:E], ones_col_b[:, 0:1], wxt_b[:, kb, :],
                           start=(kb == 0), stop=(kb == 7))
        return ins
    pl.stage("tensor", t_wsum, waits=["wl_cast", "v_setup"], inc="sT", tag="wsum_mm")

    def v_wsum_rows(v):
        v.tensor_copy(out=rowsb[0:1, 0, 0:E], in_=pst0[0:1, 0:E])
        return v.tensor_copy(out=rowsb[0:1, 1, 0:RK], in_=pst0[0:1, 0:RK])
    pl.stage("vector", v_wsum_rows, waits=["wsum_mm"], inc="sV", tag="wsum_rows")

    def t_wsum_bc(t):
        return t.matmul(pst1[:, 0:E], ones_row_b[0:1, :], rowsb[0:1, 0, 0:E],
                        start=True, stop=True)
    pl.stage("tensor", t_wsum_bc, waits=["wsum_rows"], inc="sT", tag="wsum_bc_mm")

    def v_wsum_bc(v):
        return v.tensor_copy(out=wsum96[:, :], in_=pst1[:, 0:E])
    pl.stage("vector", v_wsum_bc, waits=["wsum_bc_mm"], inc="sV", tag="wsum_bc")

    def t_wv_tr(t):
        return t.transpose(pdb0[0:RK, 0:1], rowsb[0:1, 1, 0:RK], identb[0:1, 0:1])
    pl.stage("tensor", t_wv_tr, waits=["wsum_rows", "v_setup"], inc="sT", tag="wv_tr")

    def v_wv_col(v):
        return v.tensor_copy(out=colsb[0:RK, 0:1], in_=pdb0[0:RK, 0:1])
    pl.stage("vector", v_wv_col, waits=["wv_tr"], inc="sV", tag="wv_col")

    def t_wv_mm(t):
        return t.matmul(pst0[0:1, 0:DQ], colsb[0:RK, 0:1], wdt_b[:, :],
                        start=True, stop=True)
    pl.stage("tensor", t_wv_mm, waits=["wv_col", "wl_cast"], inc="sT", tag="wv_mm")

    def v_wv_rows(v):
        v.tensor_copy(out=rowsb[0:1, 2, :], in_=pst0[0:1, 0:P])
        return v.tensor_copy(out=rowsb[0:1, 3, :], in_=pst0[0:1, P:DQ])
    pl.stage("vector", v_wv_rows, waits=["wv_mm"], inc="sV", tag="wv_rows")

    def t_wv_bc(t):
        t.matmul(pst1[:, 0:P], ones_row_b[0:1, :], rowsb[0:1, 2, :],
                 start=True, stop=True)
        return t.matmul(pst1[:, P:DQ], ones_row_b[0:1, :], rowsb[0:1, 3, :],
                        start=True, stop=True)
    pl.stage("tensor", t_wv_bc, waits=["wv_rows", "wsum_bc"], inc="sT", tag="wv_bc_mm")

    def v_wv_bc(v):
        return v.tensor_copy(out=wvbc[:, :], in_=pst1[:, 0:DQ])
    pl.stage("vector", v_wv_bc, waits=["wv_bc_mm"], inc="sV", tag="wv_bc")

    for k in range(NBLK):
        K = f"k{k}"
        pk = f"k{k-1}"
        m1_pend = [None]
        for c4 in range(NCH):
          for ti in range(c4 * 4, c4 * 4 + 4):
            rg = ti % 2
            tt = ti % 4

            w = []
            if ti >= 2:
                w = [f"{K}_ucast_{ti-2}", f"{K}_usq_{ti-2}"]
            elif k > 0:
                w = [f"{pk}_ucast_{LT-2+ti}", f"{pk}_usq_{LT-2+ti}"]

            def s_uload(s, k=k, ti=ti, rg=rg):
                return s.dma_start(out=u_tm[:, rg, :],
                                   in_=u_b[k * TB + ti * P: k * TB + (ti + 1) * P, :])
            pl.stage("sync", s_uload, waits=w, inc=f"dU{ti % 2}", dma=True, tag=f"{K}_uload_{ti}")

            w2 = [f"{K}_ln1_{ti-2}"] if ti >= 2 else ([f"{pk}_ln1_{LT-2+ti}"] if k else [])

            def s_uqload(s, k=k, ti=ti, rg=rg):
                return s.dma_start(out=uq_t[:, rg, :],
                                   in_=u_q[k * TB + ti * P: k * TB + (ti + 1) * P, :])
            pl.stage("sync", s_uqload, waits=w2, inc=f"dQ{ti % 2}", dma=True, tag=f"{K}_uqload_{ti}")

            ucw = [f"{K}_uload_{ti}", "v_setup"]
            if ti < 2 and k > 0:
                ucw.append(f"{pk}_attn_done")
            if tt == 0:
                if c4 >= 1:
                    ucw.append(f"{K}_xbar_{c4-1}")
                elif k > 0:
                    ucw.append(f"{pk}_xbar_3")

            def s_ucs(s, ti=ti, rg=rg, tt=tt):
                return s.activation(out=u_bfq[:, tt, :], in_=u_tm[:, rg, :],
                                    func=AF.Copy, accum_out=mcol[:, ti:ti + 1])
            pl.stage("scalar", s_ucs, waits=ucw, inc="sS", tag=f"{K}_ucast_{ti}")

            sq_w = [f"{K}_uload_{ti}"]
            if ti < 2 and k > 0:
                sq_w.append(f"{pk}_scan_done")

            def s_usq(s, ti=ti, rg=rg):
                return s.activation(out=absl[:, rg, :], in_=u_tm[:, rg, :],
                                    func=AF.Square, accum_out=rcol[:, ti:ti + 1])
            pl.stage("scalar", s_usq, waits=sq_w, inc="sS",
                     tag=f"{K}_usq_{ti}")

            def v_ustat(v, ti=ti):
                v.tensor_scalar(out=mcol[:, ti:ti + 1], in0=mcol[:, ti:ti + 1],
                                scalar1=1.0 / D, scalar2=None, op0=OP.mult)
                i2r = v.tensor_scalar(out=rcol[:, ti:ti + 1], in0=rcol[:, ti:ti + 1],
                                      scalar1=1.0 / D, scalar2=None, op0=OP.mult)
                fence(v, i2r)
                return v.scalar_tensor_tensor(out=rcol[:, ti:ti + 1],
                                              in0=mcol[:, ti:ti + 1],
                                              scalar=mcol[:, ti:ti + 1],
                                              in1=rcol[:, ti:ti + 1],
                                              op0=OP.mult, op1=OP.subtract)
            pl.stage("vector", v_ustat, waits=[f"{K}_ucast_{ti}", f"{K}_usq_{ti}"],
                     inc="sV", tag=f"{K}_ustat_{ti}")

            def emit_tail(ti=ti, rg=rg, k=k, K=K, pk=pk):
                def s_urstd(s, ti=ti):
                    s.activation(out=rcol[:, ti:ti + 1], in_=rcol[:, ti:ti + 1],
                                 func=AF.Ln, scale=consts[:, 2:3], bias=consts[:, 0:1])
                    return s.activation(out=rcol[:, ti:ti + 1], in_=rcol[:, ti:ti + 1],
                                        func=AF.Exp, scale=consts[:, 3:4])
                pl.stage("scalar", s_urstd, waits=[f"{K}_ustat_{ti}"], inc="sS",
                         tag=f"{K}_urstd_{ti}")

                def v_nmr(v, ti=ti):
                    return v.scalar_tensor_tensor(out=nmrcol[:, ti:ti + 1],
                                                  in0=mcol[:, ti:ti + 1], scalar=-1.0,
                                                  in1=rcol[:, ti:ti + 1],
                                                  op0=OP.mult, op1=OP.mult)
                pl.stage("vector", v_nmr, waits=[f"{K}_urstd_{ti}"], inc="sV",
                         tag=f"{K}_nmr_{ti}")

                lw = [f"{K}_uqload_{ti}", f"{K}_nmr_{ti}"]
                if k > 0 and ti < 2:
                    lw.append(f"{pk}_attn_done")
                    lw.append(f"{pk}_dtuT")

                def v_ln1(v, ti=ti, rg=rg):
                    v.tensor_scalar(out=ln1s[:, rg, :], in0=uq_t[:, rg, :],
                                    scalar1=rcol[:, ti:ti + 1],
                                    scalar2=nmrcol[:, ti:ti + 1],
                                    op0=OP.mult, op1=OP.add)
                    v.tensor_tensor(out=ln1d[:, ti, :], in0=ln1s[:, rg, :],
                                    in1=dbc_s[:, :], op=OP.mult)
                    return v.tensor_copy(out=dtu[:, ti, :], in_=ln1s[:, rg, :])
                pl.stage("vector", v_ln1, waits=lw, inc="sV", tag=f"{K}_ln1_{ti}")

            if m1_pend[0] is not None:
                m1_pend[0]()
            m1_pend[0] = emit_tail
            if ti == LT - 1:
                m1_pend[0]()
                m1_pend[0] = None

          if True:
            xw = [f"{K}_ucast_{c4 * 4 + tt2}" for tt2 in range(4)]
            if c4 >= 1:
                xw.append(f"{K}_xdmm_{c4-1}_7")
            elif k > 0:
                xw.append(f"{pk}_xdmm_3_7")

            def s_xbar(s, c4=c4):
                return s.dma_start_transpose(
                    out=xbuf[:, :, :, :].rearrange("p a b c -> p (a b c)")
                    .rearrange("p (a b) -> p a b", b=P),
                    in_=u_bfq[:, :, :].rearrange("p a b -> p (a b)")
                    .rearrange("p (a b) -> p a b", b=P))
            pl.stage("sync", s_xbar, waits=xw, inc="dXB", dma=True,
                     tag=f"{K}_xbar_{c4}")

            for d8 in range(8):
                tw = [f"{K}_xbar_{c4}", "wsum_bc_mm"]
                if d8 == 0:
                    if c4 >= 1:
                        tw.append(f"{K}_xdev_{c4-1}")
                    elif k > 0:
                        tw.append(f"{pk}_xdev_3")
                    else:
                        tw.append("wv_bc_mm")

                def t_xdmm(t, c4=c4, d8=d8):
                    return t.matmul(pst0[0:E, :], wxt_b[:, d8, :], xbuf[:, :, d8, :],
                                    start=(d8 == 0), stop=(d8 == 7))
                pl.stage("tensor", t_xdmm, waits=tw, inc="sT", tag=f"{K}_xdmm_{c4}_{d8}")

            vw = [f"{K}_xdmm_{c4}_7"]
            if k > 0 and c4 == 0:
                vw.append(f"{pk}_bctr_done")

            def s_xdev(s, c4=c4):
                return s.activation(out=xdbl_fm[:, c4 * CH:(c4 + 1) * CH],
                                    in_=pst0[0:E, :], func=AF.Copy)
            pl.stage("scalar", s_xdev, waits=vw, inc="sS", tag=f"{K}_xdev_{c4}")

        def v_dtraw(v):
            return v.tensor_copy(out=dtraw_fm[:, :], in_=xdbl_fm[0:RK, :])
        pl.stage("vector", v_dtraw, waits=[f"{K}_xdev_3"], inc="sV", tag=f"{K}_dtraw")

        for ti in range(LT):
            bw = [f"{K}_xdev_{ti // 4}", "wl_plain"]
            if ti >= 1:
                bw.append(f"{K}_bcfix_{ti-1}")
            elif k > 0:
                bw.append(f"{pk}_bcfix_{LT-1}")

            def t_bctr(t, ti=ti):
                return t.transpose(pst1[:, 0:32],
                                   xdbl_fm[RK:E, ti * P:(ti + 1) * P],
                                   identf[RK:E, RK:E])
            pl.stage("tensor", t_bctr, waits=bw, inc="sT", tag=f"{K}_bctr_{ti}")

            fw = [f"{K}_bctr_{ti}", f"{K}_nmr_{ti}", "wsum_bc", "wv_bc"]
            if k > 0 and ti < 1:
                fw.append(f"{pk}_ccast_done")

            def v_bcfix(v, ti=ti):
                v.tensor_scalar(out=xdbl_tm[:, ti, :], in0=pst1[:, 0:32],
                                scalar1=rcol[:, ti:ti + 1], scalar2=None, op0=OP.mult)
                i2 = v.scalar_tensor_tensor(out=xdbl_tm[:, ti, :], in0=wsum96[:, RK:E],
                                            scalar=nmrcol[:, ti:ti + 1],
                                            in1=xdbl_tm[:, ti, :], op0=OP.mult, op1=OP.add)
                fence(v, i2)
                return v.tensor_scalar(out=ccast[:, ti, :], in0=xdbl_tm[:, ti, :],
                                       scalar1=1.0, scalar2=None, op0=OP.mult)
            pl.stage("vector", v_bcfix, waits=fw, inc="sV", tag=f"{K}_bcfix_{ti}")
        pl.alias(f"{K}_bctr_done", f"{K}_bctr_{LT-1}")

        for tp in range(0, LT, 2):
            dw = [f"{K}_dtraw", "wl_cast"]
            if tp >= 2:
                dw.append(f"{K}_dtfix1_{tp-2}")
            elif k > 0:
                dw.append(f"{pk}_dtfix1_{LT-2}")

            def t_dtmm(t, tp=tp):
                t.matmul(pst0[:, 0:DQ], dtraw_fm[:, tp * P:(tp + 1) * P],
                         wdt_b[:, :], start=True, stop=True)
                return t.matmul(pst0[:, DQ:2 * DQ],
                                dtraw_fm[:, (tp + 1) * P:(tp + 2) * P],
                                wdt_b[:, :], start=True, stop=True)
            pl.stage("tensor", t_dtmm, waits=dw, inc="sT", tag=f"{K}_dtmm_{tp}")

            tw0 = [f"{K}_dtmm_{tp}"]
            if k > 0 and tp < 2:
                tw0.append(f"{pk}_dttr_done")
                tw0.append(f"{pk}_a2a_wr")

            def v_dtfix1(v, tp=tp):
                for j in range(2):
                    ins = v.tensor_scalar(out=tmbuf[:, tp + j, :],
                                          in0=pst0[:, j * DQ:(j + 1) * DQ],
                                          scalar1=rcol[:, tp + j:tp + j + 1],
                                          scalar2=None, op0=OP.mult)
                return ins
            pl.stage("vector", v_dtfix1, waits=tw0, inc="sV", tag=f"{K}_dtfix1_{tp}")

            def v_dtfix2(v, tp=tp):
                for j in range(2):
                    v.scalar_tensor_tensor(out=tmbuf[:, tp + j, :], in0=wvbc[:, :],
                                           scalar=nmrcol[:, tp + j:tp + j + 1],
                                           in1=tmbuf[:, tp + j, :],
                                           op0=OP.mult, op1=OP.add)
                    ins = v.tensor_tensor(out=tmbuf[:, tp + j, :],
                                          in0=tmbuf[:, tp + j, :],
                                          in1=bdtbc_s[:, :], op=OP.add)
                return ins
            pl.stage("vector", v_dtfix2, waits=[], inc="sV", tag=f"{K}_dtfix2_{tp}")

            spw = [f"{K}_dtfix2_{tp}"]
            if tp >= 2:
                spw.append(f"{K}_dtsp_{tp-2}")

            def s_sp(s, tp=tp):
                s.activation(out=spbuf[:, :, :], in_=tmbuf[:, tp:tp + 2, :],
                             func=AF.Abs)
                s.activation(out=spbuf[:, :, :], in_=spbuf[:, :, :],
                             func=AF.Exp, scale=consts[:, 2:3])
                return s.activation(out=spbuf[:, :, :], in_=spbuf[:, :, :],
                                    func=AF.Ln, bias=consts[:, 1:2])
            pl.stage("scalar", s_sp, waits=spw, inc="sS", tag=f"{K}_sp_{tp}")

            def v_dtsp(v, tp=tp):
                return v.scalar_tensor_tensor(out=tmbuf[:, tp:tp + 2, :],
                                              in0=tmbuf[:, tp:tp + 2, :],
                                              scalar=0.0, in1=spbuf[:, :, :],
                                              op0=OP.max, op1=OP.add)
            pl.stage("vector", v_dtsp, waits=[f"{K}_sp_{tp}"], inc="sV",
                     tag=f"{K}_dtsp_{tp}")

            def v_dtu(v, tp=tp):
                return v.tensor_tensor(out=dtu[:, tp:tp + 2, :],
                                       in0=dtu[:, tp:tp + 2, :],
                                       in1=tmbuf[:, tp:tp + 2, :], op=OP.mult)
            pl.stage("vector", v_dtu, waits=[f"{K}_ln1_{tp+1}", f"{K}_dtsp_{tp}"],
                     inc="sV", tag=f"{K}_dtu_{tp+1}")

            for ti in (tp, tp + 1):
              for db2 in range(2):
                trw = [f"{K}_dtsp_{tp}"]
                if ti >= 1 or db2 == 1:
                    prev = f"{K}_dtev_{ti}_{db2-1}" if db2 == 1 else f"{K}_dtev_{ti-1}_1"
                    trw.append(prev)
                elif k > 0:
                    trw.append(f"{pk}_dtev_{LT-1}_1")

                def t_dttr(t, ti=ti, db2=db2):
                    return t.transpose(pst1[:, 0:P], tmbuf[:, ti, db2 * P:(db2 + 1) * P],
                                       identf[:, :])
                pl.stage("tensor", t_dttr, waits=trw, inc="sT", tag=f"{K}_dttr_{ti}_{db2}")

                ew = [f"{K}_dttr_{ti}_{db2}"]
                if k > 0 and ti == 0 and db2 == 0:
                    ew.append(f"{pk}_abdone")

                def s_dtev(s, ti=ti, db2=db2):
                    return s.activation(out=dt_fm[:, db2, ti * P:(ti + 1) * P],
                                        in_=pst1[:, 0:P], func=AF.Copy)
                pl.stage("scalar", s_dtev, waits=ew, inc="sS", tag=f"{K}_dtev_{ti}_{db2}")
        pl.alias(f"{K}_dttr_done", f"{K}_dttr_{LT-1}_1")

        if dbg:
            def s_dbg_ln1e(s, k=k):
                return s.dma_start(out=d_ln1e.ap()[k].rearrange("(a p) c -> p a c", p=P),
                                   in_=ln1d[:, :, :])
            pl.stage("sync", s_dbg_ln1e, waits=[f"{K}_ln1_{LT-1}"], inc="dSP",
                     dma=True, tag=f"{K}_dbg_ln1e")

            def s_dbg_m2(s, k=k):
                return [s.dma_start(out=d_xdbl.ap()[k].rearrange("(a p) c -> p a c", p=P),
                                    in_=xdbl_tm[:, :, :]),
                        s.dma_start(out=d_dt.ap()[k].rearrange("(a p) c -> p a c", p=P),
                                    in_=tmbuf[:, :, :]),
                        s.dma_start(out=d_dtu.ap()[k].rearrange("(a p) c -> p a c", p=P),
                                    in_=dtu[:, :, :]),
                        s.dma_start(out=d_ln1d.ap()[k].rearrange("(a p) c -> p a c", p=P),
                                    in_=ln1d[:, :, :]),
                        s.dma_start(out=d_ccast.ap()[k].rearrange("(a p) c -> p a c", p=P),
                                    in_=ccast[:, :, :]),
                        s.dma_start(out=d_stats.ap()[k, :, 0:16], in_=mcol[:, :]),
                        s.dma_start(out=d_stats.ap()[k, :, 16:32], in_=rcol[:, :]),
                        s.dma_start(out=d_stats.ap()[k, :, 32:48], in_=nmrcol[:, :])]
            s_dbg_m2.dma_n = 8
            pl.stage("sync", s_dbg_m2,
                     waits=[f"{K}_dtsp_{LT-2}", f"{K}_dtu_{LT-1}", f"{K}_bcfix_{LT-1}",
                            f"{K}_ln1_{LT-1}"],
                     inc="dSP", dma=True, tag=f"{K}_dbg_m2")

        for ti in range(LT):
            cw = [f"{K}_bcfix_{ti}", "v_setup"]
            if ti >= 1:
                cw.append(f"{K}_crow_{ti-1}")
            elif k > 0:
                cw.append(f"{pk}_crow_{LT-1}")

            def t_ctr(t, ti=ti):
                return t.transpose(pdb0[0:2 * NST, 0:P], ccast[:, ti, :], identb[:, :])
            pl.stage("tensor", t_ctr, waits=cw, inc="sT", tag=f"{K}_ctr_{ti}")

            crw = [f"{K}_ctr_{ti}"]
            if k > 0 and ti < 1:
                crw.append(f"{pk}_crow_hbm")

            def v_crow(v, ti=ti):
                return v.tensor_copy(out=crows[:, ti, :], in_=pdb0[0:2 * NST, 0:P])
            pl.stage("vector", v_crow, waits=crw, inc="sV", tag=f"{K}_crow_{ti}")

        pl.alias(f"{K}_ccast_done", f"{K}_ctr_{LT-1}")

        def s_crow_hbm(s, k=k):
            return s.dma_start(out=crows_hbm.ap()[k, :, :],
                               in_=crows[:, :, :].rearrange("n t p -> n (t p)"))
        pl.stage("sync", s_crow_hbm, waits=[f"{K}_crow_{LT-1}"], inc="dCR", dma=True,
                 tag=f"{K}_crow_hbm")

        ftw = [f"{K}_dtu_{LT-1}"]
        if k > 0:
            ftw.append(f"{pk}_scan_1_{NST-1}_1")

        def s_dtuT(s):
            return s.dma_start_transpose(
                out=dtuT[:, :, :, :].rearrange("p a b c -> p (a b c)")
                .rearrange("p (a b) -> p a b", b=P),
                in_=dtu[:, :, :].rearrange("p a b -> p (a b)")
                .rearrange("p (a b) -> p a b", b=P))
        pl.stage("sync", s_dtuT, waits=ftw, inc="dFT", dma=True, tag=f"{K}_dtuT")

        for H in range(2):
            for n in range(NST):
                sl = n % 2
                hn = H * NST + n

                cbw = [f"{K}_crow_hbm"]
                if hn >= 2:
                    cbw.append(f"{K}_scan_{H}_{n-2}_1" if n >= 2
                               else f"{K}_scan_{H-1}_{NST-2+n}_1")
                elif k > 0:
                    cbw.append(f"{pk}_scan_1_{NST-2+hn}_1")

                def s_cbc(s, k=k, H=H, n=n, sl=sl):
                    return s.dma_start(
                        out=cbc[:, sl, :],
                        in_=crows_hbm.ap()[k, NST + n:NST + n + 1,
                                           H * 1024:(H + 1) * 1024]
                        .broadcast_to((P, 1024)))
                pl.stage("gpsimd", s_cbc, waits=cbw, inc=f"dCB{n % 2}", dma=True,
                         tag=f"{K}_cbc_{H}_{n}")

                def s_bbc(s, k=k, H=H, n=n, sl=sl):
                    return s.dma_start(
                        out=bbc[:, sl, :],
                        in_=crows_hbm.ap()[k, n:n + 1, H * 1024:(H + 1) * 1024]
                        .broadcast_to((P, 1024)))
                pl.stage("gpsimd", s_bbc, waits=list(cbw), inc=f"dBB{n % 2}", dma=True,
                         tag=f"{K}_bbc_{H}_{n}")

                for db2 in range(2):
                    idx = db2 * NST + n
                    gi = hn * 2 + db2
                    asl = gi % 2

                    aw = [f"{K}_dtev_{LT-1}_1", "acols_neg"]
                    if gi >= 2:
                        aw.append(f"{K}_scan_{H}_{n-1}_{db2}" if n >= 1
                                  else f"{K}_scan_{H-1}_{NST-1}_{db2}")
                    elif k > 0:
                        aw.append(f"{pk}_scan_1_{NST-1}_{db2}")

                    def s_ab(s, H=H, db2=db2, n=n, asl=asl):
                        return s.activation(out=absl[:, asl, :],
                                            in_=dt_fm[:, db2, H * 1024:(H + 1) * 1024],
                                            func=AF.Exp,
                                            scale=acols[:, db2 * NST + n:db2 * NST + n + 1])
                    pl.stage("scalar", s_ab, waits=aw, inc="sS",
                             tag=f"{K}_ab_{H}_{n}_{db2}")

                    sw = [f"{K}_ab_{H}_{n}_{db2}", f"{K}_dtuT",
                          f"{K}_cbc_{H}_{n}", f"{K}_bbc_{H}_{n}"]
                    if gi >= 2:
                        sw.append(f"{K}_ymm_{H}_{n-1}_{db2}" if n >= 1
                                  else f"{K}_ymm_{H-1}_{NST-1}_{db2}")
                    elif k > 0:
                        sw.append(f"{pk}_ymm_1_{NST-1}_{db2}")

                    def v_scan(v, H=H, db2=db2, n=n, sl=sl, asl=asl, idx=idx):
                        v.tensor_tensor(
                            out=dbT[:, :].rearrange("p (a b) -> p a b", b=P),
                            in0=dtuT[:, H * 8:(H + 1) * 8, db2, :],
                            in1=bbc[:, sl, :].rearrange("p (a b) -> p a b", b=P),
                            op=OP.mult)
                        v.tensor_tensor_scan(out=hsl[:, :], data0=absl[:, asl, :],
                                             data1=dbT[:, :],
                                             initial=statec[:, idx:idx + 1],
                                             op0=OP.mult, op1=OP.add)
                        v.tensor_tensor(out=tsl[:, asl, :], in0=hsl[:, :],
                                        in1=cbc[:, sl, :], op=OP.mult)
                        return v.tensor_copy(out=statec[:, idx:idx + 1],
                                             in_=hsl[:, 1023:1024])
                    pl.stage("vector", v_scan, waits=sw, inc="sV",
                             tag=f"{K}_scan_{H}_{n}_{db2}")

                    yw = [f"{K}_scan_{H}_{n}_{db2}"]
                    if n == 0:
                        if H > 0:
                            yw.append(f"{K}_yev_{H-1}_{db2}_1")
                        elif k > 0:
                            yw.append(f"{pk}_yev_1_{db2}_1")

                    def t_ymm(t, H=H, db2=db2, n=n, asl=asl):
                        t.matmul(py[db2 * 2 + 0][:, :], identb[:, :], tsl[:, asl, 0:CH],
                                 start=(n == 0), stop=(n == NST - 1))
                        return t.matmul(py[db2 * 2 + 1][:, :], identb[:, :],
                                        tsl[:, asl, CH:1024],
                                        start=(n == 0), stop=(n == NST - 1))
                    pl.stage("tensor", t_ymm, waits=yw, inc="sT",
                             tag=f"{K}_ymm_{H}_{n}_{db2}")

            for db2 in range(2):
                for c01 in range(2):
                    yw2 = [f"{K}_ymm_{H}_{NST-1}_{db2}"]
                    prev_ev = None
                    if c01 == 1:
                        prev_ev = (H, db2, 0)
                    elif db2 == 1:
                        prev_ev = (H, 0, 1)
                    if prev_ev is not None:
                        yw2.append(f"{K}_ytr_{prev_ev[0]}_{prev_ev[1]}_{prev_ev[2]}_3")

                    def s_yev_c(s, db2=db2, c01=c01):
                        return s.activation(out=ybuf[:, :], in_=py[db2 * 2 + c01][:, :],
                                            func=AF.Copy)
                    pl.stage("scalar", s_yev_c, waits=yw2, inc="sS",
                             tag=f"{K}_yev_{H}_{db2}_{c01}")

                    for lt4 in range(4):
                        ti = H * 8 + c01 * 4 + lt4
                        trw = [f"{K}_yev_{H}_{db2}_{c01}"]
                        seq = []
                        for hh in range(2):
                            for dd in range(2):
                                for cc in range(2):
                                    for ll in range(4):
                                        seq.append((hh, dd, cc, ll))
                        cur = seq.index((H, db2, c01, lt4))
                        if cur > 1:
                            ph, pd, pc, plt = seq[cur - 2]
                            trw.append(f"{K}_attn_{ph}_{pd}_{pc}_{plt}")
                        elif k > 0:
                            trw.append(f"{pk}_attn_1_1_1_{2+cur}")

                        pty = pst1 if cur % 2 == 0 else pst0

                        def t_ytr(t, lt4=lt4, pty=pty):
                            return t.transpose(pty[:, 0:P],
                                               ybuf[:, lt4 * P:(lt4 + 1) * P],
                                               identf[:, :])
                        pl.stage("tensor", t_ytr, waits=trw, inc="sT",
                                 tag=f"{K}_ytr_{H}_{db2}_{c01}_{lt4}")

                        atw = [f"{K}_ytr_{H}_{db2}_{c01}_{lt4}", f"{K}_ln1_{ti}",
                               f"{K}_dttr_done", f"{K}_sp_{(ti // 2) * 2}"] \
                            + ([f"{K}_dbg_m2"] if dbg else [])
                        if k > 0 and cur == 0:
                            atw.append(f"{pk}_a2a_wr")

                        def v_attn(v, ti=ti, db2=db2, pty=pty):
                            return v.tensor_tensor(out=tmbuf[:, ti, db2 * P:(db2 + 1) * P],
                                                   in0=ln1d[:, ti, db2 * P:(db2 + 1) * P],
                                                   in1=pty[:, 0:P], op=OP.add)
                        pl.stage("vector", v_attn, waits=atw, inc="sV",
                                 tag=f"{K}_attn_{H}_{db2}_{c01}_{lt4}")

        pl.alias(f"{K}_scan_done", f"{K}_scan_1_{NST-1}_1")
        pl.alias(f"{K}_abdone", f"{K}_ab_1_{NST-1}_1")
        pl.alias(f"{K}_attn_done", f"{K}_attn_1_1_1_3")


        @dma_n(8)
        def s_a2a_wr(s, k=k):
            ins = []
            for j in range(8):
                st4 = j % 4
                ins.append(s.dma_start(
                    out=a2a_in[k].ap()[j * CH:(j + 1) * CH, :]
                    .rearrange("(a p) c -> p a c", p=P),
                    in_=tmbuf[:, 4 * st4:4 * st4 + 4, :]))
            return ins
        pl.stage("sync", s_a2a_wr, waits=[f"{K}_attn_done"], inc="dAW", dma=True,
                 tag=f"{K}_a2a_wr")

        if dbg:
            def s_dbg_attn(s, k=k):
                return s.dma_start(out=d_attn.ap()[k].rearrange("(a p) c -> p a c", p=P),
                                   in_=tmbuf[:, :, :])
            pl.stage("sync", s_dbg_attn, waits=[f"{K}_attn_done"], inc="dSP", dma=True,
                     tag=f"{K}_dbg_attn")

        def g_a2a(g, k=k):
            g.wait_ge_ext = None
            with allow_ici_all_to_all():
                return g.collective_compute(
                    "AllToAll", OP.bypass, replica_groups=[list(range(8))],
                    ins=[a2a_in[k].ap().opt()], outs=[a2a_out[k].ap().opt()])
        pl.stage("gpsimd", g_a2a, waits=[f"{K}_a2a_wr"], inc="sCC", tag=f"{K}_a2a")

    mixer_done = [pl.last["vector"], pl.last["scalar"], pl.last["tensor"],
                  pl.last["gpsimd"]]
    gf_first = [True]
    for k in range(NBLK):
        K = f"A{k}"
        kw_base = [f"k{k}_a2a"] + ([f"A{k-1}_tfsp"] if k else mixer_done)
        for ti in range(4):
            @dma_n(2)
            def s_fxl(s, k=k, ti=ti):
                i1 = s.dma_start(out=fx2[:, ti, :].rearrange("p (a b) -> p a b", b=DQ),
                                 in_=a2a_out[k].ap()
                                 .rearrange("(s a p) c -> p a s c", a=4, p=P)[:, ti, 0:4, :])
                i2 = s.dma_start(out=fln2[:, ti, :].rearrange("p (a b) -> p a b", b=DQ),
                                 in_=a2a_out[k].ap()
                                 .rearrange("(s a p) c -> p a s c", a=4, p=P)[:, ti, 4:8, :])
                return [i1, i2]
            if k == 0:
                lw = list(kw_base)
            else:
                lw = [f"k{k}_a2a", f"A{k-1}_x2sp_{ti}", f"A{k-1}_ln2sp_{ti}"]
            pl.stage("gpsimd", s_fxl, waits=lw, inc=f"dFL{ti}", dma=True,
                     tag=f"{K}_ld_{ti}")

        def emit_ul(ti, k=k, K=K):
            def s_ul(s, k=k, ti=ti):
                return s.dma_start(out=u_tm[:, ti % 2, :],
                                   in_=u_tok[k, ti * P:(ti + 1) * P, :])
            if ti >= 2:
                uw = [f"{K}_x2_{ti-2}"]
            elif k == 0:
                uw = list(kw_base)
            else:
                uw = [f"A{k-1}_x2_{ti+2}"]
            pl.stage("sync", s_ul, waits=uw, inc=f"dUS{ti % 2}", dma=True,
                     tag=f"{K}_ul_{ti}")
        emit_ul(0)
        emit_ul(1)

        for ti in range(4):
            def v_asm(v, ti=ti):
                v.tensor_scalar(out=fx2[:, ti, :], in0=fx2[:, ti, :],
                                scalar1=masks[:, 0:1], scalar2=None, op0=OP.mult)
                return v.scalar_tensor_tensor(out=fx2[:, ti, :], in0=fln2[:, ti, :],
                                              scalar=masks[:, 1:2], in1=fx2[:, ti, :],
                                              op0=OP.mult, op1=OP.add)
            pl.stage("vector", v_asm, waits=[f"{K}_ld_{ti}"], inc="sV",
                     tag=f"{K}_asm_{ti}")

            def v_s1(v, ti=ti):
                return v.reduce_sum(out=m2col[:, ti:ti + 1], in_=fx2[:, ti, :], axis=AX.X)
            pl.stage("vector", v_s1, waits=[f"{K}_asm_{ti}"], inc="sV", tag=f"{K}_s1_{ti}")

            def s_s2(s, ti=ti):
                return s.activation(out=fbf[:, ti, :], in_=fx2[:, ti, :],
                                    func=AF.Square, accum_out=r2col[:, ti:ti + 1])
            s2w = [f"{K}_asm_{ti}", f"{K}_s1_{ti}"]
            if k > 0 and ti == 0:
                s2w += [f"A{k-1}_fxbar", f"A{k-1}_gf_3"]
            pl.stage("scalar", s_s2, waits=s2w, inc="sS", tag=f"{K}_s2_{ti}")

            def v_s3(v, ti=ti):
                v.tensor_scalar(out=m2col[:, ti:ti + 1], in0=m2col[:, ti:ti + 1],
                                scalar1=1.0 / D, scalar2=None, op0=OP.mult)
                i2r = v.tensor_scalar(out=r2col[:, ti:ti + 1], in0=r2col[:, ti:ti + 1],
                                      scalar1=1.0 / D, scalar2=None, op0=OP.mult)
                fence(v, i2r)
                return v.scalar_tensor_tensor(out=r2col[:, ti:ti + 1],
                                              in0=m2col[:, ti:ti + 1],
                                              scalar=m2col[:, ti:ti + 1],
                                              in1=r2col[:, ti:ti + 1],
                                              op0=OP.mult, op1=OP.subtract)
            pl.stage("vector", v_s3, waits=[f"{K}_s2_{ti}"], inc="sV", tag=f"{K}_s3_{ti}")

            def s_s4(s, ti=ti):
                s.activation(out=r2col[:, ti:ti + 1], in_=r2col[:, ti:ti + 1],
                             func=AF.Ln, scale=consts[:, 2:3], bias=consts[:, 0:1])
                return s.activation(out=r2col[:, ti:ti + 1], in_=r2col[:, ti:ti + 1],
                                    func=AF.Exp, scale=consts[:, 3:4])
            pl.stage("scalar", s_s4, waits=[f"{K}_s3_{ti}"], inc="sS", tag=f"{K}_s4_{ti}")

            def v_s5(v, ti=ti):
                return v.scalar_tensor_tensor(out=nm2col[:, ti:ti + 1],
                                              in0=m2col[:, ti:ti + 1], scalar=-1.0,
                                              in1=r2col[:, ti:ti + 1],
                                              op0=OP.mult, op1=OP.mult)
            pl.stage("vector", v_s5, waits=[f"{K}_s4_{ti}"], inc="sV", tag=f"{K}_s5_{ti}")

            def v_x2(v, ti=ti):
                v.tensor_scalar(out=fx2[:, ti, :], in0=fx2[:, ti, :],
                                scalar1=r2col[:, ti:ti + 1], scalar2=nm2col[:, ti:ti + 1],
                                op0=OP.mult, op1=OP.add)
                return v.tensor_tensor(out=fx2[:, ti, :], in0=fx2[:, ti, :],
                                       in1=u_tm[:, ti % 2, :], op=OP.add)
            pl.stage("vector", v_x2, waits=[f"{K}_s5_{ti}", f"{K}_ul_{ti}"],
                     inc="sV", tag=f"{K}_x2_{ti}")
            if ti + 2 < 4:
                emit_ul(ti + 2)

            def s_x2sp(s, k=k, ti=ti):
                return s.dma_start(out=x2_sp.ap()[k, ti * P:(ti + 1) * P, :],
                                   in_=fx2[:, ti, :])
            pl.stage("sync", s_x2sp, waits=[f"{K}_x2_{ti}"], inc=f"dXS{k}", dma=True,
                     tag=f"{K}_x2sp_{ti}")

            def v_t1(v, ti=ti):
                return v.reduce_sum(out=m2col[:, ti:ti + 1], in_=fx2[:, ti, :], axis=AX.X)
            pl.stage("vector", v_t1, waits=[f"{K}_x2_{ti}"], inc="sV", tag=f"{K}_t1_{ti}")

            def s_t2(s, ti=ti):
                return s.activation(out=fbf[:, ti, :], in_=fx2[:, ti, :],
                                    func=AF.Square, accum_out=r2col[:, ti:ti + 1])
            pl.stage("scalar", s_t2, waits=[f"{K}_t1_{ti}"], inc="sS", tag=f"{K}_t2_{ti}")

            def v_t3(v, ti=ti):
                v.tensor_scalar(out=m2col[:, ti:ti + 1], in0=m2col[:, ti:ti + 1],
                                scalar1=1.0 / D, scalar2=None, op0=OP.mult)
                i2r = v.tensor_scalar(out=r2col[:, ti:ti + 1], in0=r2col[:, ti:ti + 1],
                                      scalar1=1.0 / D, scalar2=None, op0=OP.mult)
                fence(v, i2r)
                return v.scalar_tensor_tensor(out=r2col[:, ti:ti + 1],
                                              in0=m2col[:, ti:ti + 1],
                                              scalar=m2col[:, ti:ti + 1],
                                              in1=r2col[:, ti:ti + 1],
                                              op0=OP.mult, op1=OP.subtract)
            pl.stage("vector", v_t3, waits=[f"{K}_t2_{ti}"], inc="sV", tag=f"{K}_t3_{ti}")

            def s_t4(s, ti=ti):
                s.activation(out=r2col[:, ti:ti + 1], in_=r2col[:, ti:ti + 1],
                             func=AF.Ln, scale=consts[:, 2:3], bias=consts[:, 0:1])
                return s.activation(out=r2col[:, ti:ti + 1], in_=r2col[:, ti:ti + 1],
                                    func=AF.Exp, scale=consts[:, 3:4])
            pl.stage("scalar", s_t4, waits=[f"{K}_t3_{ti}"], inc="sS", tag=f"{K}_t4_{ti}")

            def v_t5(v, ti=ti):
                i2 = v.scalar_tensor_tensor(out=nm2col[:, ti:ti + 1],
                                            in0=m2col[:, ti:ti + 1], scalar=-1.0,
                                            in1=r2col[:, ti:ti + 1],
                                            op0=OP.mult, op1=OP.mult)
                fence(v, i2)
                v.tensor_scalar(out=fln2[:, ti, :], in0=fx2[:, ti, :],
                                scalar1=r2col[:, ti:ti + 1], scalar2=nm2col[:, ti:ti + 1],
                                op0=OP.mult, op1=OP.add)
                return v.tensor_copy(out=fbf[:, ti, :], in_=fln2[:, ti, :])
            pl.stage("vector", v_t5, waits=[f"{K}_t4_{ti}"], inc="sV",
                     tag=f"{K}_ln2c_{ti}")

            def s_ln2sp(s, k=k, ti=ti):
                return s.dma_start(out=ln2_sp.ap()[k, ti * P:(ti + 1) * P, :],
                                   in_=fln2[:, ti, :])
            pl.stage("sync", s_ln2sp, waits=[f"{K}_ln2c_{ti}"], inc=f"dLS{k}", dma=True,
                     tag=f"{K}_ln2sp_{ti}")

            gfw = [f"{K}_ln2c_{ti}"]
            if not gf_first[0]:
                pass
            def t_gf(t, ti=ti, first=(gf_first[0] and ti == 0), k=k):
                st = (k == 0 and ti == 0)
                sp = (k == NBLK - 1 and ti == 3)
                t.matmul(py[0][0:1, :], ones_col_b[:, 0:1], fbf[:, ti, 0:CH],
                         start=st, stop=sp)
                return t.matmul(py[1][0:1, :], ones_col_b[:, 0:1], fbf[:, ti, CH:D],
                                start=st, stop=sp)
            pl.stage("tensor", t_gf, waits=gfw, inc="sT", tag=f"{K}_gf_{ti}")
        gf_first[0] = False

        xw = [f"{K}_ln2c_{t2}" for t2 in range(4)]
        if k > 0:
            xw.append(f"A{k-1}_h1_mm_7")

        def s_fxbar(s):
            return s.dma_start_transpose(
                out=ffm[:, :, :, :].rearrange("p a b c -> p (a b c)")
                .rearrange("p (a b) -> p a b", b=P),
                in_=fbf[:, :, :].rearrange("p a b -> p (a b)")
                .rearrange("p (a b) -> p a b", b=P))
        pl.stage("sync", s_fxbar, waits=xw, inc="dFX", dma=True, tag=f"{K}_fxbar")

        def ffn_mms(K, src_fm, wsrc, nk, nh, outbuf, bcol, psA, psB, wtag, wait_extra):
            for hb in range(nh):
                mw = [wtag] + (wait_extra if hb == 0 else [])
                if hb >= 2:
                    mw.append(f"{K}_{outbuf}_ge_{hb-2}")

                def t_mm(t, hb=hb, src_fm=src_fm, wsrc=wsrc, nk=nk, psA=psA, psB=psB):
                    ps = psA if hb % 2 == 0 else psB
                    ins = None
                    for kb in range(nk):
                        ins = t.matmul(ps[:, :], wsrc[:, kb, hb, :], src_fm(kb),
                                       start=(kb == 0), stop=(kb == nk - 1))
                    return ins
                pl.stage("tensor", t_mm, waits=mw, inc="sT", tag=f"{K}_{outbuf}_mm_{hb}")

                def s_ge(s, hb=hb, outbuf=outbuf, bcol=bcol, psA=psA, psB=psB):
                    ps = psA if hb % 2 == 0 else psB
                    ob = {"h1": fh1, "h2": fh2, "h3": fh3}[outbuf]
                    return s.activation(out=ob[:, hb, :], in_=ps[:, :], func=AF.Gelu,
                                        bias=bcol[:, hb:hb + 1])
                pl.stage("scalar", s_ge, waits=[f"{K}_{outbuf}_mm_{hb}"], inc="sS",
                         tag=f"{K}_{outbuf}_ge_{hb}")

        we = ["wl_cast"] + ([f"A{k-1}_tfsp"] if k > 0 else [])
        ffn_mms(K, lambda kb: ffm[:, :, kb, :], w1b, 8, 8, "h1", b1c, pst0, pst1,
                f"{K}_fxbar", we)
        ffn_mms(K, lambda kb: fh1[:, kb, :], w2b, 8, 4, "h2", b2c, pst0, pst1,
                f"{K}_h1_ge_7", [])
        ffn_mms(K, lambda kb: fh2[:, kb, :], w3b, 4, 2, "h3", b3c, pst0, pst1,
                f"{K}_h2_ge_3", [])

        def t_tf(t):
            t.matmul(pst0[0:1, :], w4b[:, 0:1], fh3[:, 0, :], start=True, stop=False)
            return t.matmul(pst0[0:1, :], w4b[:, 1:2], fh3[:, 1, :],
                            start=False, stop=True)
        pl.stage("tensor", t_tf, waits=[f"{K}_h3_ge_1", f"{K}_h1_mm_7" if False else f"{K}_h3_ge_1"],
                 inc="sT", tag=f"{K}_tfmm")

        def s_tf(s):
            return s.activation(out=frow[0:1, :], in_=pst0[0:1, :], func=AF.Tanh,
                                scale=consts[0:1, 4:5], bias=bias4[0:1, 0:1])
        pl.stage("scalar", s_tf, waits=[f"{K}_tfmm", "v_halfb4"], inc="sS",
                 tag=f"{K}_tfrow")

        def s_tfsp(s, k=k):
            return s.dma_start(out=tf0_sp.ap()[k:k + 1, :], in_=frow[0:1, :])
        pl.stage("sync", s_tfsp, waits=[f"{K}_tfrow"], inc=f"dTS{k}", dma=True,
                 tag=f"{K}_tfsp")

    def v_gfev(v):
        v.tensor_copy(out=gmr[0:1, 0:CH], in_=py[0][0:1, :])
        v.tensor_copy(out=gmr[0:1, CH:D], in_=py[1][0:1, :])
        return v.tensor_scalar(out=gmr[0:1, :], in0=gmr[0:1, :], scalar1=1.0 / L,
                               scalar2=None, op0=OP.mult)
    pl.stage("vector", v_gfev, waits=[f"A{NBLK-1}_gf_3"], inc="sV", tag="gfev")

    def s_gfar_wr(s):
        return s.dma_start(out=gfar_in.ap()[0:1, :], in_=gmr[0:1, :])
    pl.stage("sync", s_gfar_wr, waits=["gfev"], inc="dGW", dma=True, tag="gfar_wr")

    def g_ar(g):
        return g.collective_compute("AllReduce", OP.add,
                                    replica_groups=[[0, 1, 2, 3], [4, 5, 6, 7]],
                                    ins=[gfar_in.ap().opt()], outs=[gfar_out.ap().opt()])
    pl.stage("gpsimd", g_ar, waits=["gfar_wr", f"k{NBLK-1}_a2a"], inc="sCC", tag="gf_ar")

    def s_gfar_rd(s):
        return s.dma_start(out=gmr[0:1, :], in_=gfar_out.ap()[0:1, :])
    pl.stage("sync", s_gfar_rd, waits=["gf_ar"], inc="dGR", dma=True, tag="gfar_rd")

    if dbg:
        def s_dbg_gm(s):
            return [s.dma_start(out=d_gm.ap()[0:1, :], in_=gfar_in.ap()[0:1, :]),
                    s.dma_start(out=d_gm.ap()[1:2, :], in_=gfar_out.ap()[0:1, :])]
        s_dbg_gm.dma_n = 2
        pl.stage("sync", s_dbg_gm, waits=["gfar_rd"], inc="dSP", dma=True, tag="dbg_gm")

    def g_gcw(g):
        i1 = g.dma_start(out=gw1b[:, :, :, :], in_=gc_w1)
        return [i1, g.dma_start(out=gw2b[:, :, :, :], in_=gc_w2)]
    g_gcw.dma_n = 2
    pl.stage("gpsimd", g_gcw, waits=[f"k{NBLK-1}_a2a", f"A{NBLK-1}_h1_ge_7"],
             inc="dG", dma=True, tag="gcw_ld")

    def t_gmtr(t):
        ins = None
        for j in range(8):
            ins = t.transpose(pst1[:, j:j + 1],
                              gmr[0:1, j * P:(j + 1) * P], identf[0:1, 0:1])
        return ins
    pl.stage("tensor", t_gmtr, waits=["gfar_rd", f"A{NBLK-1}_tfsp"], inc="sT",
             tag="gmtr_7")

    def v_gmc(v):
        return v.tensor_copy(out=gmc[:, 0:8], in_=pst1[:, 0:8])
    pl.stage("vector", v_gmc, waits=["gmtr_7"], inc="sV", tag="gmc_7")

    for hb in range(4):
        def t_g1(t, hb=hb):
            ins = None
            for kb in range(8):
                ins = t.matmul(pst0[:, hb:hb + 1], gw1b[:, kb, hb, :], gmc[:, kb:kb + 1],
                               start=(kb == 0), stop=(kb == 7))
            return ins
        pl.stage("tensor", t_g1, waits=["gmc_7", "gcw_ld"], inc="sT", tag=f"g1_{hb}")
    for hb in range(4):
        def s_g1c(s, hb=hb):
            return s.activation(out=gfh1[:, hb:hb + 1], in_=pst0[:, hb:hb + 1],
                                func=AF.Gelu, bias=gb1c[:, hb:hb + 1])
        pl.stage("scalar", s_g1c, waits=["g1_3"], inc="sS", tag=f"g1c_{hb}")

    for hb in range(2):
        def t_g2(t, hb=hb):
            ins = None
            for kb in range(4):
                ins = t.matmul(pst1[:, hb:hb + 1], gw2b[:, kb, hb, :], gfh1[:, kb:kb + 1],
                               start=(kb == 0), stop=(kb == 3))
            return ins
        pl.stage("tensor", t_g2, waits=["g1c_3", "gmc_7"], inc="sT", tag=f"g2_{hb}")
    for hb in range(2):
        def s_g2c(s, hb=hb):
            return s.activation(out=gfh2[:, hb:hb + 1], in_=pst1[:, hb:hb + 1],
                                func=AF.Gelu, bias=gb2c[:, hb:hb + 1])
        pl.stage("scalar", s_g2c, waits=["g2_1"], inc="sS", tag=f"g2c_{hb}")

    def t_g3(t):
        t.matmul(pst0[0:1, 0:1], gw3b[:, 0:1], gfh2[:, 0:1], start=True, stop=False)
        return t.matmul(pst0[0:1, 0:1], gw3b[:, 1:2], gfh2[:, 1:2],
                        start=False, stop=True)
    pl.stage("tensor", t_g3, waits=["g2c_1", "g1c_3"], inc="sT", tag="g3")

    def s_g3(s):
        return s.activation(out=abcol[0:1, 2:3], in_=pst0[0:1, 0:1], func=AF.Tanh,
                            scale=consts[0:1, 4:5], bias=bias4[0:1, 1:2])
    pl.stage("scalar", s_g3, waits=["g3", "v_halfb4"], inc="sS", tag="g3act")

    def v_coef(v):
        i1 = v.scalar_tensor_tensor(out=abcol[0:1, 2:3], in0=abcol[0:1, 2:3], scalar=1.0,
                                    in1=abcol[0:1, 3:4], op0=OP.mult, op1=OP.mult)
        fence(v, i1)
        v.tensor_scalar(out=abcol[0:1, 1:2], in0=abcol[0:1, 2:3], scalar1=0.7,
                        scalar2=0.1, op0=OP.mult, op1=OP.add)
        return v.tensor_scalar(out=abcol[0:1, 0:1], in0=abcol[0:1, 3:4], scalar1=0.3,
                               scalar2=None, op0=OP.mult)
    pl.stage("vector", v_coef, waits=["g3act", "wl_plain"], inc="sV", tag="coef_row")

    def v_coefb(v):
        return v.tensor_copy(out=rowsb[0:1, 4, 0:2], in_=abcol[0:1, 0:2])
    pl.stage("vector", v_coefb, waits=["coef_row"], inc="sV", tag="coef_rowb")

    def t_coef_bc(t):
        return t.matmul(pst0[:, 0:2], ones_row_b[0:1, :], rowsb[0:1, 4, 0:2],
                        start=True, stop=True)
    pl.stage("tensor", t_coef_bc, waits=["coef_rowb", "g3"], inc="sT", tag="coef_bc")

    def v_coefc(v):
        return v.tensor_copy(out=abcol[:, 0:2], in_=pst0[:, 0:2])
    pl.stage("vector", v_coefc, waits=["coef_bc"], inc="sV", tag="coef")

    fobs = [fob, fln2]

    def b_ld(k):
        sl = k % 2

        @dma_n(2)
        def s_bld(s, k=k, sl=sl):
            i1 = s.dma_start(out=fobs[sl][:, :, :],
                             in_=ln2_sp.ap()[k].rearrange("(a p) c -> p a c", p=P))
            i2 = s.dma_start(out=tfc[:, k, :],
                             in_=tf0_sp.ap()[k].rearrange("(a p) -> p a", p=P),
                             allow_slow_non_contiguous=True)
            return [i1, i2]
        bw = [f"A{k}_ln2sp_3", f"A{k}_tfsp"]
        if k == 1:
            bw.append(f"A{NBLK-1}_ln2sp_3")
        if k >= 2:
            bw.append(f"B{k-2}_out")
        pl.stage("sync", s_bld, waits=bw, inc="dBL", dma=True, tag=f"B{k}_ld")

    CHN = [0]
    prev_chain = [None]

    def b_cycle(k, r):
        K = f"B{k}"
        sl = k % 2
        fo = fobs[sl]
        cw = [f"{K}_ld", "coef"] if r == 0 else [f"{K}_c{r}_tfc"]

        def v_cc(v, k=k):
            ic = None
            for tcol in range(4):
                ic = v.scalar_tensor_tensor(out=ccol[:, tcol:tcol + 1],
                                            in0=tfc[:, k, tcol:tcol + 1],
                                            scalar=abcol[:, 0:1],
                                            in1=abcol[:, 1:2],
                                            op0=OP.mult, op1=OP.add)
            fence(v, ic)
            return v.tensor_scalar(out=ncc[:, :], in0=ccol[:, :], scalar1=-1.0,
                                   scalar2=None, op0=OP.mult)
        pl.stage("vector", v_cc, waits=cw, inc="sV", tag=f"{K}_c{r}_cc")

        for ti in range(4):
            def v_tw(v, ti=ti, fo=fo):
                q = [fo[:, ti, 0:DQ], fo[:, ti, DQ:2 * DQ],
                     fo[:, ti, 2 * DQ:3 * DQ], fo[:, ti, 3 * DQ:D]]
                v.tensor_copy(out=ytmp[:, 0, :], in_=q[0])
                v.tensor_copy(out=ytmp[:, 1, :], in_=q[1])
                v.scalar_tensor_tensor(out=q[0], in0=q[2], scalar=ccol[:, ti:ti + 1],
                                       in1=q[0], op0=OP.mult, op1=OP.add)
                v.scalar_tensor_tensor(out=q[1], in0=q[3], scalar=ncc[:, ti:ti + 1],
                                       in1=q[1], op0=OP.mult, op1=OP.add)
                v.scalar_tensor_tensor(out=q[2], in0=ytmp[:, 0, :],
                                       scalar=ncc[:, ti:ti + 1], in1=q[2],
                                       op0=OP.mult, op1=OP.add)
                return v.scalar_tensor_tensor(out=q[3], in0=ytmp[:, 1, :],
                                              scalar=ccol[:, ti:ti + 1], in1=q[3],
                                              op0=OP.mult, op1=OP.add)
            pl.stage("vector", v_tw, waits=[f"{K}_c{r}_cc"], inc="sV",
                     tag=f"{K}_c{r}_tw_{ti}")

        if r >= 2:
            return None

        KC = f"{K}_c{r}"
        ci = CHN[0]
        CHN[0] += 1
        prev = prev_chain[0]

        def v_cast(v, fo=fo):
            ins = None
            for ti in range(4):
                ins = v.tensor_copy(out=fbf[:, ti, :], in_=fo[:, ti, :])
            return ins
        caw = [f"{K}_c{r}_tw_3"]
        if prev is None:
            caw += [f"A{NBLK-1}_fxbar", f"A{NBLK-1}_gf_3"]
        else:
            caw.append(f"{prev}_xbar")
        pl.stage("vector", v_cast, waits=caw, inc="sV", tag=f"{KC}_cast")

        xw = [f"{KC}_cast",
              f"{prev}_h1_mm_7" if prev else f"A{NBLK-1}_h1_mm_7"]

        def s_bxbar(s):
            return s.dma_start_transpose(
                out=ffm[:, :, :, :].rearrange("p a b c -> p (a b c)")
                .rearrange("p (a b) -> p a b", b=P),
                in_=fbf[:, :, :].rearrange("p a b -> p (a b)")
                .rearrange("p (a b) -> p a b", b=P))
        pl.stage("sync", s_bxbar, waits=xw, inc="dBX", dma=True, tag=f"{KC}_xbar")

        we = [f"{prev}_tfrow"] if prev else ["coef"]
        ffn_mms(KC, lambda kb: ffm[:, :, kb, :], w1b, 8, 8, "h1", b1c,
                pst0, pst1, f"{KC}_xbar", we)
        ffn_mms(KC, lambda kb: fh1[:, kb, :], w2b, 8, 4, "h2", b2c,
                pst0, pst1, f"{KC}_h1_ge_7", [])
        ffn_mms(KC, lambda kb: fh2[:, kb, :], w3b, 4, 2, "h3", b3c,
                pst0, pst1, f"{KC}_h2_ge_3", [])

        tmw = [f"{KC}_h3_ge_1"]
        if prev:
            tmw.append(f"{prev}_tfrow")

        def t_btf(t):
            t.matmul(pst0[0:1, :], w4b[:, 0:1], fh3[:, 0, :], start=True,
                     stop=False)
            return t.matmul(pst0[0:1, :], w4b[:, 1:2], fh3[:, 1, :],
                            start=False, stop=True)
        pl.stage("tensor", t_btf, waits=tmw, inc="sT", tag=f"{KC}_tfmm")

        def s_btf(s):
            return s.activation(out=frow[0:1, :], in_=pst0[0:1, :], func=AF.Tanh,
                                scale=consts[0:1, 4:5], bias=bias4[0:1, 0:1])
        pl.stage("scalar", s_btf, waits=[f"{KC}_tfmm"], inc="sS",
                 tag=f"{KC}_tfrow")

        pyt = py[ci % 2]
        trw = [f"{KC}_tfrow"]
        trw.append(CHAIN_TFC[ci - 2] if ci >= 2 else "gfev")

        def t_tftr(t, pyt=pyt):
            ins = None
            for ti in range(4):
                ins = t.transpose(pyt[:, ti:ti + 1],
                                  frow[0:1, ti * P:(ti + 1) * P],
                                  identf[0:1, 0:1])
            return ins
        pl.stage("tensor", t_tftr, waits=trw, inc="sT", tag=f"{KC}_tftr")
        prev_chain[0] = KC
        return (KC, k, r, pyt)

    CHAIN_TFC = {}

    def b_tfc(ch):
        KC, k, r, pyt = ch

        def v_tfc(v, k=k, pyt=pyt):
            return v.tensor_copy(out=tfc[:, k, :], in_=pyt[:, 0:4])
        pl.stage("vector", v_tfc, waits=[f"{KC}_tftr"], inc="sV",
                 tag=f"{KC}_tfc_3")
        pl.alias(f"B{k}_c{r+1}_tfc", f"{KC}_tfc_3")
        ci = CHN[0] - 1
        for idx, (kk, rr) in enumerate(CHAIN_IDS):
            if (kk, rr) == (k, r):
                CHAIN_TFC[idx] = f"{KC}_tfc_3"

    CHAIN_IDS = []

    def b_final(k):
        K = f"B{k}"
        sl = k % 2
        fo = fobs[sl]

        def s_fx2ld(s, k=k):
            return s.dma_start(out=fx2[:, :, :],
                               in_=x2_sp.ap()[k].rearrange("(a p) c -> p a c", p=P))
        fw = [f"A{k}_x2sp_3"]
        if k == 0:
            fw += [f"A{NBLK-1}_ln2c_3", f"A{NBLK-1}_x2sp_3"]
        else:
            fw.append(f"B{k-1}_p5_3")
        pl.stage("sync", s_fx2ld, waits=fw, inc="dX2", dma=True, tag=f"{K}_fx2ld")

        for ti in range(4):
            def v_p1(v, ti=ti, fo=fo):
                return v.reduce_sum(out=m2col[:, ti:ti + 1], in_=fo[:, ti, :],
                                    axis=AX.X)
            pl.stage("vector", v_p1, waits=[f"{K}_c2_tw_{ti}"], inc="sV",
                     tag=f"{K}_p1_{ti}")

            def s_p2(s, ti=ti, fo=fo):
                return s.activation(out=hsl[:, :], in_=fo[:, ti, :],
                                    func=AF.Square, accum_out=r2col[:, ti:ti + 1])
            pl.stage("scalar", s_p2, waits=[f"{K}_p1_{ti}"], inc="sS",
                     tag=f"{K}_p2_{ti}")

            def v_p3(v, ti=ti):
                v.tensor_scalar(out=m2col[:, ti:ti + 1], in0=m2col[:, ti:ti + 1],
                                scalar1=1.0 / D, scalar2=None, op0=OP.mult)
                i2r = v.tensor_scalar(out=r2col[:, ti:ti + 1], in0=r2col[:, ti:ti + 1],
                                      scalar1=1.0 / D, scalar2=None, op0=OP.mult)
                fence(v, i2r)
                return v.scalar_tensor_tensor(out=r2col[:, ti:ti + 1],
                                              in0=m2col[:, ti:ti + 1],
                                              scalar=m2col[:, ti:ti + 1],
                                              in1=r2col[:, ti:ti + 1],
                                              op0=OP.mult, op1=OP.subtract)
            pl.stage("vector", v_p3, waits=[f"{K}_p2_{ti}"], inc="sV",
                     tag=f"{K}_p3_{ti}")

            def s_p4(s, ti=ti):
                s.activation(out=r2col[:, ti:ti + 1], in_=r2col[:, ti:ti + 1],
                             func=AF.Ln, scale=consts[:, 2:3], bias=consts[:, 0:1])
                return s.activation(out=r2col[:, ti:ti + 1], in_=r2col[:, ti:ti + 1],
                                    func=AF.Exp, scale=consts[:, 3:4])
            pl.stage("scalar", s_p4, waits=[f"{K}_p3_{ti}"], inc="sS",
                     tag=f"{K}_p4_{ti}")

            def v_p5(v, ti=ti, fo=fo):
                i2 = v.scalar_tensor_tensor(out=nm2col[:, ti:ti + 1],
                                            in0=m2col[:, ti:ti + 1], scalar=-1.0,
                                            in1=r2col[:, ti:ti + 1],
                                            op0=OP.mult, op1=OP.mult)
                fence(v, i2)
                v.tensor_scalar(out=fo[:, ti, :], in0=fo[:, ti, :],
                                scalar1=r2col[:, ti:ti + 1], scalar2=nm2col[:, ti:ti + 1],
                                op0=OP.mult, op1=OP.add)
                return v.tensor_tensor(out=fo[:, ti, :], in0=fo[:, ti, :],
                                       in1=fx2[:, ti, :], op=OP.add)
            pl.stage("vector", v_p5, waits=[f"{K}_p4_{ti}", f"{K}_fx2ld"],
                     inc="sV", tag=f"{K}_p5_{ti}")

        def s_out(s, k=k, fo=fo):
            return s.dma_start(out=out_tok[k].rearrange("(a p) c -> p a c", p=P),
                               in_=fo[:, :, :])
        pl.stage("sync", s_out, waits=[f"{K}_p5_3"], inc="dOUT", dma=True,
                 tag=f"{K}_out")

    for p0 in range(0, NBLK, 2):
        pr = list(range(p0, min(p0 + 2, NBLK)))
        if len(pr) == 2:
            a, b = pr
            b_ld(a)
            b_ld(b)
            CHAIN_IDS += [(a, 0), (b, 0), (a, 1), (b, 1)]
            ch_a0 = b_cycle(a, 0)
            ch_b0 = b_cycle(b, 0)
            b_tfc(ch_a0)
            ch_a1 = b_cycle(a, 1)
            b_tfc(ch_b0)
            ch_b1 = b_cycle(b, 1)
            b_tfc(ch_a1)
            b_cycle(a, 2)
            b_final(a)
            b_tfc(ch_b1)
            b_cycle(b, 2)
            b_final(b)
        else:
            a = pr[0]
            b_ld(a)
            CHAIN_IDS += [(a, 0), (a, 1)]
            ch_a0 = b_cycle(a, 0)
            b_tfc(ch_a0)
            ch_a1 = b_cycle(a, 1)
            b_tfc(ch_a1)
            b_cycle(a, 2)
            b_final(a)

    final_out = pl.cnt["dOUT"]

    sems = {}
    for name in pl.cnt:
        sems[name] = ctx.enter_context(nc.semaphore(name))
    for e in ENGS:
        FENCE["sems"][e] = ctx.enter_context(nc.semaphore(f"fence_{e}"))
        FENCE["cnt"][e] = 0

    block = ctx.enter_context(nc.Block())

    def emit(eng_name, stream):
        FENCE["eng"] = eng_name
        for waits, fn, inc, dma, gidx, snap, seramt in pl.items[eng_name]:
            if max_stages is not None and gidx > max_stages:
                continue
            if serialize:
                for sn, sv in snap.items():
                    if sv > 0:
                        stream.wait_ge(sems[sn], sv)
            for w in waits:
                sname, val = pl.marks[w]
                stream.wait_ge(sems[sname], val)
            ret = fn(stream)
            if inc is not None:
                if dma:
                    insts = ret if isinstance(ret, list) else [ret]
                    for i in insts:
                        i.then_inc(sems[inc], 16)
                else:
                    ret.then_inc(sems[inc], 1)

    @block.sync
    def _(sync):
        emit("sync", sync)
        if max_stages is None:
            sync.wait_ge(sems["dOUT"], final_out)

    @block.vector
    def _(vector):
        emit("vector", vector)

    @block.scalar
    def _(scalar):
        emit("scalar", scalar)

    @block.tensor
    def _(tensor):
        emit("tensor", tensor)

    @block.gpsimd
    def _(gpsimd):
        emit("gpsimd", gpsimd)

    ctx.close()
    return nc


_NC_CACHE = {}


def _get_nc():
    if "nc" not in _NC_CACHE:
        _NC_CACHE["nc"] = build_program(4)
    return _NC_CACHE["nc"]


def kernel(L0_attn_u, L0_attn_A_log, L0_attn_W_x, L0_attn_W_dt, L0_attn_b_dt,
           L0_attn_D_param, L0_ffn_cn_w1, L0_ffn_cn_b1, L0_ffn_cn_w2, L0_ffn_cn_b2,
           L0_ffn_cn_w3, L0_ffn_cn_b3, L0_ffn_cn_w4, L0_ffn_cn_b4, L0_ffn_gc_w1,
           L0_ffn_gc_b1, L0_ffn_gc_w2, L0_ffn_gc_b2, L0_ffn_gc_w3, L0_ffn_gc_b3,
           L0_ffn_adaptive_range):
    f32 = np.float32
    u = np.ascontiguousarray(np.asarray(L0_attn_u, dtype=f32))
    A_log = np.asarray(L0_attn_A_log, f32)
    W_x = np.asarray(L0_attn_W_x, f32)
    W_dt = np.asarray(L0_attn_W_dt, f32)
    b_dt = np.asarray(L0_attn_b_dt, f32)
    D_param = np.asarray(L0_attn_D_param, f32)

    def c(a):
        return np.ascontiguousarray(a, dtype=f32)

    w1h = c(np.asarray(L0_ffn_cn_w1, f32).reshape(8, P, 8, P).transpose(1, 0, 2, 3))
    w2h = c(np.asarray(L0_ffn_cn_w2, f32).reshape(8, P, 4, P).transpose(1, 0, 2, 3))
    w3h = c(np.asarray(L0_ffn_cn_w3, f32).reshape(4, P, 2, P).transpose(1, 0, 2, 3))
    w4h = c(np.asarray(L0_ffn_cn_w4, f32).reshape(2, P).T)
    b1h = c(np.asarray(L0_ffn_cn_b1, f32).reshape(8, P).T)
    b2h = c(np.asarray(L0_ffn_cn_b2, f32).reshape(4, P).T)
    b3h = c(np.asarray(L0_ffn_cn_b3, f32).reshape(2, P).T)
    gw1h = c(np.asarray(L0_ffn_gc_w1, f32).reshape(8, P, 4, P).transpose(1, 0, 2, 3))
    gw2h = c(np.asarray(L0_ffn_gc_w2, f32).reshape(4, P, 2, P).transpose(1, 0, 2, 3))
    gw3h = c(np.asarray(L0_ffn_gc_w3, f32).reshape(2, P).T)
    gb1h = c(np.asarray(L0_ffn_gc_b1, f32).reshape(4, P).T)
    gb2h = c(np.asarray(L0_ffn_gc_b2, f32).reshape(2, P).T)
    b4h = c(np.asarray(L0_ffn_cn_b4, f32).reshape(1, 1))
    gb3h = c(np.asarray(L0_ffn_gc_b3, f32).reshape(1, 1))
    arh = c(np.asarray(L0_ffn_adaptive_range, f32).reshape(1, 1))
    wxth = c(W_x.T)
    identh = np.eye(P, dtype=f32)

    in_maps = []
    for core in range(8):
        b = core // 4
        qq = core % 4
        aqh = c(A_log[DQ * qq:DQ * (qq + 1)].reshape(2, P, NST)
                .transpose(1, 0, 2).reshape(P, 32))
        m0 = 1.0 if b == 0 else 0.0
        mh = np.empty((P, 2), f32)
        mh[:, 0] = m0
        mh[:, 1] = 1.0 - m0
        utokh = np.stack([u[b, 2048 * k + CH * qq: 2048 * k + CH * (qq + 1), :]
                          for k in range(NBLK)])
        in_maps.append(dict(
            u_b=u[b],
            u_q=c(u[b][:, DQ * qq:DQ * (qq + 1)]),
            u_tok=c(utokh),
            wxt=wxth,
            wdtq=c(W_dt[DQ * qq:DQ * (qq + 1), :].T),
            aq=aqh,
            bdtbc=c(np.broadcast_to(b_dt[DQ * qq:DQ * (qq + 1)], (P, DQ))),
            dbc=c(np.broadcast_to(D_param[DQ * qq:DQ * (qq + 1)], (P, DQ))),
            masks=mh,
            ident_in=identh,
            cn_w1=w1h, cn_w2=w2h, cn_w3=w3h, cn_w4=w4h,
            cn_b1=b1h, cn_b2=b2h, cn_b3=b3h, cn_b4=b4h,
            gc_w1=gw1h, gc_w2=gw2h, gc_w3=gw3h,
            gc_b1=gb1h, gc_b2=gb2h, gc_b3=gb3h,
            arr=arh,
        ))

    nc = _get_nc()
    res = run_bass_kernel_spmd(nc, in_maps, core_ids=list(range(8)))

    out = np.empty((2, L, D), dtype=f32)
    for core in range(8):
        b = core // 4
        qq = core % 4
        ot = np.asarray(res.results[core]["out_tok"])
        for k in range(NBLK):
            out[b, 2048 * k + CH * qq: 2048 * k + CH * (qq + 1), :] = ot[k]
    return out


if __name__ == "__main__":
    import time
    t0 = time.time()
    nc = build_program()
    n_inst = sum(1 for _ in nc.all_instructions())
    print(f"built in {time.time()-t0:.1f}s, {n_inst} instructions")


# version 15
# speedup vs baseline: 2.1996x; 1.0386x over previous
import numpy as np
import ml_dtypes
from contextlib import ExitStack

import concourse.bass as bass
import concourse.mybir as mybir
from concourse.bass_utils import run_bass_kernel_spmd
from concourse.aot_env import allow_ici_all_to_all

F32 = mybir.dt.float32
BF16 = mybir.dt.bfloat16
AF = mybir.ActivationFunctionType
OP = mybir.AluOpType
AX = mybir.AxisListType

P = 128
L = 8192
D = 1024
DQ = 256
NST = 16
E = 96
RK = 64
NBLK = 4
TB = 2048
LT = TB // P
NCH = 4
CH = 512
EPS = 1e-5

ENGS = ("sync", "vector", "scalar", "tensor", "gpsimd")


class Plan:
    def __init__(self):
        self.items = {e: [] for e in ENGS}
        self.cnt = {}
        self.marks = {}
        self.last = {e: None for e in ENGS}

    def sem(self, name):
        if name not in self.cnt:
            self.cnt[name] = 0

    def stage(self, eng, fn, waits=(), inc=None, dma=False, tag=None):
        for w in waits:
            assert w in self.marks, f"stage tag={tag}: unknown wait {w}"
        self.gidx = getattr(self, "gidx", 0) + 1
        self.ser_before = getattr(self, "ser_after", 0)
        amt = 16 * getattr(fn, "dma_n", 1) if dma else 1
        self.ser_after = self.ser_before + amt
        self.items[eng].append((list(waits), fn, inc, dma, self.gidx,
                                dict(self.cnt), amt))
        if inc is not None:
            self.sem(inc)
            if dma:
                self.cnt[inc] += 16 * getattr(fn, "dma_n", 1)
            else:
                self.cnt[inc] += 1
        if tag is not None:
            assert inc is not None, tag
            self.marks[tag] = (inc, self.cnt[inc])
            self.last[eng] = tag

    def alias(self, tag, src):
        self.marks[tag] = self.marks[src]

FENCE = {"cnt": {}, "sems": {}, "eng": None}

def fence(stream, inst):
    e = FENCE["eng"]
    FENCE["cnt"][e] += 1
    inst.then_inc(FENCE["sems"][e], 1)
    stream.wait_ge(FENCE["sems"][e], FENCE["cnt"][e])


def dma_n(n):
    def deco(fn):
        fn.dma_n = n
        return fn
    return deco


def build_program(nblk=NBLK, max_stages=None, dbg=False, serialize=False):
    global NBLK, L
    NBLK = nblk
    L = NBLK * TB
    nc = bass.Bass("TRN2", target_bir_lowering=False, debug=False, detect_race_conditions=False)

    def din(name, shape, dt=F32):
        return nc.dram_tensor(name, shape, dt, kind="ExternalInput").ap()

    u_b = din("u_b", [L, D])
    u_q = din("u_q", [L, DQ])
    u_tok = din("u_tok", [NBLK, CH, D])
    wxt = din("wxt", [D, E])
    wdtq = din("wdtq", [RK, DQ])
    aq = din("aq", [P, 32])
    bdtbc = din("bdtbc", [P, DQ])
    dbc_in = din("dbc", [P, DQ])
    masks_in = din("masks", [P, 2])
    ident_in = din("ident_in", [P, P])
    cn_w1 = din("cn_w1", [P, 8, 8, P])
    cn_w2 = din("cn_w2", [P, 8, 4, P])
    cn_w3 = din("cn_w3", [P, 4, 2, P])
    cn_w4 = din("cn_w4", [P, 2])
    cn_b1 = din("cn_b1", [P, 8])
    cn_b2 = din("cn_b2", [P, 4])
    cn_b3 = din("cn_b3", [P, 2])
    cn_b4 = din("cn_b4", [1, 1])
    gc_w1 = din("gc_w1", [P, 8, 4, P])
    gc_w2 = din("gc_w2", [P, 4, 2, P])
    gc_w3 = din("gc_w3", [P, 2])
    gc_b1 = din("gc_b1", [P, 4])
    gc_b2 = din("gc_b2", [P, 2])
    gc_b3 = din("gc_b3", [1, 1])
    arr_in = din("arr", [1, 1])

    out_tok = nc.dram_tensor("out_tok", [NBLK, CH, D], F32, kind="ExternalOutput").ap()

    dk = dict(kind="ExternalOutput") if dbg else {}
    a2a_in = [nc.dram_tensor(f"a2a_in{k}", [8 * CH, DQ], F32) for k in range(NBLK)]
    a2a_out = [nc.dram_tensor(f"a2a_out{k}", [8 * CH, DQ], F32) for k in range(NBLK)]
    gfar_in = nc.dram_tensor("gfar_in", [1, D], F32)
    gfar_out = nc.dram_tensor("gfar_out", [1, D], F32)
    d_attn = nc.dram_tensor("d_attn", [NBLK, TB, DQ], F32, **dk) if dbg else None
    d_gm = nc.dram_tensor("d_gm", [2, D], F32, **dk) if dbg else None
    d_xdbl = nc.dram_tensor("d_xdbl", [NBLK, TB, 32], F32, **dk) if dbg else None
    d_dt = nc.dram_tensor("d_dt", [NBLK, TB, DQ], F32, **dk) if dbg else None
    d_dtu = nc.dram_tensor("d_dtu", [NBLK, TB, DQ], BF16, **dk) if dbg else None
    d_ln1d = nc.dram_tensor("d_ln1d", [NBLK, TB, DQ], F32, **dk) if dbg else None
    d_stats = nc.dram_tensor("d_stats", [NBLK, P, 48], F32, **dk) if dbg else None
    d_ln1e = nc.dram_tensor("d_ln1e", [NBLK, TB, DQ], F32, **dk) if dbg else None
    d_ccast = nc.dram_tensor("d_ccast", [NBLK, TB, 2 * NST], BF16, **dk) if dbg else None
    x2_sp = nc.dram_tensor("x2_sp", [NBLK, CH, D], F32, **dk)
    ln2_sp = nc.dram_tensor("ln2_sp", [NBLK, CH, D], F32, **dk)
    tf0_sp = nc.dram_tensor("tf0_sp", [NBLK, CH], F32, **dk)
    crows_hbm = nc.dram_tensor("crows_hbm", [NBLK, 2 * NST, TB], BF16, **dk)

    ctx = ExitStack()

    def sbt(name, shape, dt=F32):
        return ctx.enter_context(nc.sbuf_tensor(name, shape, dt))

    def pst(name, shape, dt=F32):
        return ctx.enter_context(nc.psum_tensor(name, shape, dt))

    wxt_b = sbt("wxt_b", [P, 8, E], BF16)
    wdt_b = sbt("wdt_b", [RK, DQ], BF16)
    acols = sbt("acols", [P, 32])
    bdtbc_s = sbt("bdtbc_s", [P, DQ])
    dbc_s = sbt("dbc_s", [P, DQ])
    masks = sbt("masks_s", [P, 2])
    w1b = sbt("w1b", [P, 8, 8, P], BF16)
    w2b = sbt("w2b", [P, 8, 4, P], BF16)
    w3b = sbt("w3b", [P, 4, 2, P], BF16)
    w4b = sbt("w4b", [P, 2], BF16)
    b1c = sbt("b1c", [P, 8])
    b2c = sbt("b2c", [P, 4])
    b3c = sbt("b3c", [P, 2])
    bias4 = sbt("bias4", [1, 2])
    gb1c = sbt("gb1c", [P, 4])
    gb2c = sbt("gb2c", [P, 2])
    gw3b = sbt("gw3b", [P, 2], BF16)
    identb = sbt("identb", [P, P], BF16)
    identf = sbt("identf", [P, P])
    ones_col_b = sbt("ones_col_b", [P, 1], BF16)
    ones_row_b = sbt("ones_row_b", [1, P], BF16)
    ones_row_f = sbt("ones_row_f", [1, P])
    consts = sbt("consts", [P, 8])
    statec = sbt("statec", [P, 32])
    wsum96 = sbt("wsum96", [P, E])
    wvbc = sbt("wvbc", [P, DQ])
    mcol = sbt("mcol", [P, LT])
    rcol = sbt("rcol", [P, LT])
    nmrcol = sbt("nmrcol", [P, LT])
    rowsc = sbt("rowsc", [1, 8, P])
    rowsb = sbt("rowsb", [1, 8, P], BF16)
    colsc = sbt("colsc", [P, 8])
    colsb = sbt("colsb", [P, 2], BF16)

    uq_t = sbt("uq_t", [P, 4, DQ])
    ln1d = sbt("ln1d", [P, LT, DQ])
    ln1s = sbt("ln1s", [P, 2, DQ])
    spbuf = sbt("spbuf", [P, 2, DQ])
    tmbuf = sbt("tmbuf", [P, LT, DQ])
    dt_fm = sbt("dt_fm", [P, 2, TB])
    xdbl_tm = sbt("xdbl_tm", [P, LT, 32])
    xdbl_fm = sbt("xdbl_fm", [E, TB])
    dtraw_fm = sbt("dtraw_fm", [RK, TB], BF16)
    u_tm = sbt("u_tm", [P, 4, D])
    u_bfq = sbt("u_bfq", [P, 4, D], BF16)
    xbuf = sbt("xbuf", [P, 4, 8, P], BF16)
    dtu = sbt("dtu", [P, LT, DQ], BF16)
    dbsl = sbt("dbsl", [P, 2, 8, DQ], BF16)
    absl = sbt("absl", [P, 2, 1024])
    hsl = sbt("hsl", [P, 1024], BF16)
    tsl = sbt("tsl", [P, 2, 1024], BF16)
    crows = sbt("crows", [2 * NST, LT, P], BF16)
    ccast = sbt("ccast", [P, LT, 2 * NST], BF16)
    cbc = sbt("cbc", [P, 2, 1024], BF16)
    bbc = sbt("bbc", [P, 2, 1024], BF16)
    dbT = sbt("dbT", [P, 1024], BF16)
    ybuf = sbt("ybuf", [P, CH])

    fh1 = sbt("fh1", [P, 8, CH], BF16)
    frow = sbt("frow", [1, CH])
    gmr = sbt("gmr", [1, D])
    gmc = sbt("gmc", [P, 8], BF16)
    gfh1 = sbt("gfh1", [P, 4], BF16)
    gfh2 = sbt("gfh2", [P, 2], BF16)
    abcol = sbt("abcol", [P, 4])
    m2col = sbt("m2col", [P, 4])
    r2col = sbt("r2col", [P, 4])
    nm2col = sbt("nm2col", [P, 4])
    tfc = sbt("tfc", [P, NBLK, 4])
    ncc = sbt("ncc", [P, 4])
    ccol = sbt("ccol", [P, 4])

    fx2 = ln1d[:, :, :].rearrange("p a b -> p (a b)").rearrange("p (a b) -> p a b", b=D)
    fln2 = tmbuf[:, :, :].rearrange("p a b -> p (a b)").rearrange("p (a b) -> p a b", b=D)
    fob = dt_fm[:, :, :].rearrange("p a b -> p (a b)").rearrange("p (a b) -> p a b", b=D)
    fbf = absl[:, :, :].rearrange("p a b -> p (a b)").bitcast(BF16).rearrange("p (a b) -> p a b", b=D)
    ffm = dtu[:, :, :].rearrange("p a b -> p (a b)").rearrange("p (a b c) -> p a b c", b=8, c=P)
    dtuT = dbsl[:, :, :, :].rearrange("p a b c -> p (a b c)").rearrange("p (a b c) -> p a b c", b=2, c=P)
    fh2 = dbsl[:, :, :, :].rearrange("p a b c -> p (a b c)").rearrange("p (a b) -> p a b", b=CH)[:, 0:4, :]
    fh3 = hsl[:, :].rearrange("p (a b) -> p a b", b=CH)
    gw1b = u_bfq[:, :, :].rearrange("p a b -> p (a b)").rearrange("p (a b c) -> p a b c", b=4, c=P)
    gw2b = tsl[:, :, :].rearrange("p a b -> p (a b)").rearrange("p (a b c) -> p a b c", b=2, c=P)[:, 0:4, :, :]
    ytmp = ybuf[:, :].rearrange("p (a b) -> p a b", b=DQ)

    py = [pst(f"py{i}", [P, CH]) for i in range(4)]
    pdb0 = pst("pdb0", [P, 1024], BF16)
    pdb1 = pst("pdb1", [P, 1024], BF16)
    pst0 = pst("pst0", [P, CH])
    pst1 = pst("pst1", [P, CH])

    pl = Plan()

    cast_loads = [
        (wxt_b[:, :, :], wxt.rearrange("(a p) e -> p a e", p=P)),
        (wdt_b[:, :], wdtq),
        (w1b[:, :, :, :], cn_w1), (w2b[:, :, :, :], cn_w2),
        (w3b[:, :, :, :], cn_w3), (w4b[:, :], cn_w4),
        (gw3b[:, :], gc_w3),
    ]

    def g_cast_loads(g):
        ins = []
        for dst, src in cast_loads:
            ins.append(g.dma_start(out=dst, in_=src))
        return ins
    g_cast_loads.dma_n = len(cast_loads)
    pl.stage("gpsimd", g_cast_loads, inc="dG", dma=True, tag="wl_cast")

    plain_loads = [
        (bdtbc_s[:, :], bdtbc), (dbc_s[:, :], dbc_in), (masks[:, :], masks_in),
        (b1c[:, :], cn_b1), (b2c[:, :], cn_b2), (b3c[:, :], cn_b3),
        (gb1c[:, :], gc_b1), (gb2c[:, :], gc_b2),
        (identf[:, :], ident_in), (acols[:, :], aq),
        (bias4[0:1, 0:1], cn_b4), (bias4[0:1, 1:2], gc_b3),
        (abcol[0:1, 3:4], arr_in),
    ]

    def s_plain_loads(s):
        ins = []
        for dst, src in plain_loads:
            ins.append(s.dma_start(out=dst, in_=src))
        return ins
    s_plain_loads.dma_n = len(plain_loads)
    pl.stage("sync", s_plain_loads, inc="dWL", dma=True, tag="wl_plain")

    def v_setup(v):
        v.memset(consts[:, 0:1], EPS)
        v.memset(consts[:, 1:2], 1.0)
        v.memset(consts[:, 2:3], -1.0)
        v.memset(consts[:, 3:4], -0.5)
        v.memset(consts[:, 4:5], 0.5)
        v.memset(statec[:, :], 0.0)
        v.memset(ones_col_b[:, :], 1.0)
        v.memset(ones_row_b[0:1, :], 1.0)
        v.memset(ones_row_f[0:1, :], 1.0)
        return v.tensor_copy(out=identb[:, :], in_=identf[:, :])
    pl.stage("vector", v_setup, waits=["wl_plain"], inc="sV", tag="v_setup")

    def v_halfb4(v):
        return v.tensor_scalar(out=bias4[0:1, 0:2], in0=bias4[0:1, 0:2],
                               scalar1=0.5, scalar2=None, op0=OP.mult)
    pl.stage("vector", v_halfb4, waits=["wl_plain"], inc="sV", tag="v_halfb4")

    def s_acols(s):
        return s.activation(out=acols[:, :], in_=acols[:, :], func=AF.Exp)
    pl.stage("scalar", s_acols, waits=["wl_plain"], inc="sS", tag="acols_exp")

    def v_acols(v):
        return v.tensor_scalar(out=acols[:, :], in0=acols[:, :], scalar1=-1.0,
                               scalar2=None, op0=OP.mult)
    pl.stage("vector", v_acols, waits=["acols_exp"], inc="sV", tag="acols_neg")

    def t_wsum(t):
        ins = None
        for kb in range(8):
            ins = t.matmul(pst0[0:1, 0:E], ones_col_b[:, 0:1], wxt_b[:, kb, :],
                           start=(kb == 0), stop=(kb == 7))
        return ins
    pl.stage("tensor", t_wsum, waits=["wl_cast", "v_setup"], inc="sT", tag="wsum_mm")

    def v_wsum_rows(v):
        v.tensor_copy(out=rowsb[0:1, 0, 0:E], in_=pst0[0:1, 0:E])
        return v.tensor_copy(out=rowsb[0:1, 1, 0:RK], in_=pst0[0:1, 0:RK])
    pl.stage("vector", v_wsum_rows, waits=["wsum_mm"], inc="sV", tag="wsum_rows")

    def t_wsum_bc(t):
        return t.matmul(pst1[:, 0:E], ones_row_b[0:1, :], rowsb[0:1, 0, 0:E],
                        start=True, stop=True)
    pl.stage("tensor", t_wsum_bc, waits=["wsum_rows"], inc="sT", tag="wsum_bc_mm")

    def v_wsum_bc(v):
        return v.tensor_copy(out=wsum96[:, :], in_=pst1[:, 0:E])
    pl.stage("vector", v_wsum_bc, waits=["wsum_bc_mm"], inc="sV", tag="wsum_bc")

    def t_wv_tr(t):
        return t.transpose(pdb0[0:RK, 0:1], rowsb[0:1, 1, 0:RK], identb[0:1, 0:1])
    pl.stage("tensor", t_wv_tr, waits=["wsum_rows", "v_setup"], inc="sT", tag="wv_tr")

    def v_wv_col(v):
        return v.tensor_copy(out=colsb[0:RK, 0:1], in_=pdb0[0:RK, 0:1])
    pl.stage("vector", v_wv_col, waits=["wv_tr"], inc="sV", tag="wv_col")

    def t_wv_mm(t):
        return t.matmul(pst0[0:1, 0:DQ], colsb[0:RK, 0:1], wdt_b[:, :],
                        start=True, stop=True)
    pl.stage("tensor", t_wv_mm, waits=["wv_col", "wl_cast"], inc="sT", tag="wv_mm")

    def v_wv_rows(v):
        v.tensor_copy(out=rowsb[0:1, 2, :], in_=pst0[0:1, 0:P])
        return v.tensor_copy(out=rowsb[0:1, 3, :], in_=pst0[0:1, P:DQ])
    pl.stage("vector", v_wv_rows, waits=["wv_mm"], inc="sV", tag="wv_rows")

    def t_wv_bc(t):
        t.matmul(pst1[:, 0:P], ones_row_b[0:1, :], rowsb[0:1, 2, :],
                 start=True, stop=True)
        return t.matmul(pst1[:, P:DQ], ones_row_b[0:1, :], rowsb[0:1, 3, :],
                        start=True, stop=True)
    pl.stage("tensor", t_wv_bc, waits=["wv_rows", "wsum_bc"], inc="sT", tag="wv_bc_mm")

    def v_wv_bc(v):
        return v.tensor_copy(out=wvbc[:, :], in_=pst1[:, 0:DQ])
    pl.stage("vector", v_wv_bc, waits=["wv_bc_mm"], inc="sV", tag="wv_bc")

    for k in range(NBLK):
        K = f"k{k}"
        pk = f"k{k-1}"
        m1_pend = [None]
        for c4 in range(NCH):
          for ti in range(c4 * 4, c4 * 4 + 4):
            rg = ti % 4
            tt = ti % 4

            w = []
            if ti >= 4:
                w = [f"{K}_ucast_{ti-4}", f"{K}_usq_{ti-4}"]
            elif k > 0:
                w = [f"{pk}_ucast_{LT-4+ti}", f"{pk}_usq_{LT-4+ti}"]

            def s_uload(s, k=k, ti=ti, rg=rg):
                return s.dma_start(out=u_tm[:, rg, :],
                                   in_=u_b[k * TB + ti * P: k * TB + (ti + 1) * P, :])
            pl.stage("sync", s_uload, waits=w, inc=f"dU{ti % 4}", dma=True, tag=f"{K}_uload_{ti}")

            w2 = [f"{K}_ln1_{ti-4}"] if ti >= 4 else ([f"{pk}_ln1_{LT-4+ti}"] if k else [])

            def s_uqload(s, k=k, ti=ti, rg=rg):
                return s.dma_start(out=uq_t[:, rg, :],
                                   in_=u_q[k * TB + ti * P: k * TB + (ti + 1) * P, :])
            pl.stage("sync", s_uqload, waits=w2, inc=f"dQ{ti % 4}", dma=True, tag=f"{K}_uqload_{ti}")

            ucw = [f"{K}_uload_{ti}", "v_setup"]
            if ti < 4 and k > 0:
                ucw.append(f"{pk}_attn_done")
            if tt == 0:
                if c4 >= 1:
                    ucw.append(f"{K}_xbar_{c4-1}")
                elif k > 0:
                    ucw.append(f"{pk}_xbar_3")

            def s_ucs(s, ti=ti, rg=rg, tt=tt):
                return s.activation(out=u_bfq[:, tt, :], in_=u_tm[:, rg, :],
                                    func=AF.Copy, accum_out=mcol[:, ti:ti + 1])
            pl.stage("scalar", s_ucs, waits=ucw, inc="sS", tag=f"{K}_ucast_{ti}")

            sq_w = [f"{K}_uload_{ti}"]
            if ti < 4 and k > 0:
                sq_w.append(f"{pk}_scan_done")

            def s_usq(s, ti=ti, rg=rg):
                return s.activation(out=absl[:, ti % 2, :], in_=u_tm[:, rg, :],
                                    func=AF.Square, accum_out=rcol[:, ti:ti + 1])
            pl.stage("scalar", s_usq, waits=sq_w, inc="sS",
                     tag=f"{K}_usq_{ti}")

            def v_ustat(v, ti=ti):
                v.tensor_scalar(out=mcol[:, ti:ti + 1], in0=mcol[:, ti:ti + 1],
                                scalar1=1.0 / D, scalar2=None, op0=OP.mult)
                i2r = v.tensor_scalar(out=rcol[:, ti:ti + 1], in0=rcol[:, ti:ti + 1],
                                      scalar1=1.0 / D, scalar2=None, op0=OP.mult)
                fence(v, i2r)
                return v.scalar_tensor_tensor(out=rcol[:, ti:ti + 1],
                                              in0=mcol[:, ti:ti + 1],
                                              scalar=mcol[:, ti:ti + 1],
                                              in1=rcol[:, ti:ti + 1],
                                              op0=OP.mult, op1=OP.subtract)
            pl.stage("vector", v_ustat, waits=[f"{K}_ucast_{ti}", f"{K}_usq_{ti}"],
                     inc="sV", tag=f"{K}_ustat_{ti}")

            def emit_tail(ti=ti, rg=rg, k=k, K=K, pk=pk):
                def s_urstd(s, ti=ti):
                    s.activation(out=rcol[:, ti:ti + 1], in_=rcol[:, ti:ti + 1],
                                 func=AF.Ln, scale=consts[:, 2:3], bias=consts[:, 0:1])
                    return s.activation(out=rcol[:, ti:ti + 1], in_=rcol[:, ti:ti + 1],
                                        func=AF.Exp, scale=consts[:, 3:4])
                pl.stage("scalar", s_urstd, waits=[f"{K}_ustat_{ti}"], inc="sS",
                         tag=f"{K}_urstd_{ti}")

                def v_nmr(v, ti=ti):
                    return v.scalar_tensor_tensor(out=nmrcol[:, ti:ti + 1],
                                                  in0=mcol[:, ti:ti + 1], scalar=-1.0,
                                                  in1=rcol[:, ti:ti + 1],
                                                  op0=OP.mult, op1=OP.mult)
                pl.stage("vector", v_nmr, waits=[f"{K}_urstd_{ti}"], inc="sV",
                         tag=f"{K}_nmr_{ti}")

                lw = [f"{K}_uqload_{ti}", f"{K}_nmr_{ti}"]
                if k > 0 and ti < 2:
                    lw.append(f"{pk}_attn_done")
                    lw.append(f"{pk}_dtuT")

                def v_ln1(v, ti=ti, rg=rg):
                    v.tensor_scalar(out=ln1s[:, ti % 2, :], in0=uq_t[:, rg, :],
                                    scalar1=rcol[:, ti:ti + 1],
                                    scalar2=nmrcol[:, ti:ti + 1],
                                    op0=OP.mult, op1=OP.add)
                    v.tensor_tensor(out=ln1d[:, ti, :], in0=ln1s[:, ti % 2, :],
                                    in1=dbc_s[:, :], op=OP.mult)
                    return v.tensor_copy(out=dtu[:, ti, :], in_=ln1s[:, ti % 2, :])
                pl.stage("vector", v_ln1, waits=lw, inc="sV", tag=f"{K}_ln1_{ti}")

            if m1_pend[0] is not None:
                m1_pend[0]()
            m1_pend[0] = emit_tail
            if ti == LT - 1:
                m1_pend[0]()
                m1_pend[0] = None

          if True:
            xw = [f"{K}_ucast_{c4 * 4 + tt2}" for tt2 in range(4)]
            if c4 >= 1:
                xw.append(f"{K}_xdmm_{c4-1}_7")
            elif k > 0:
                xw.append(f"{pk}_xdmm_3_7")

            def s_xbar(s, c4=c4):
                return s.dma_start_transpose(
                    out=xbuf[:, :, :, :].rearrange("p a b c -> p (a b c)")
                    .rearrange("p (a b) -> p a b", b=P),
                    in_=u_bfq[:, :, :].rearrange("p a b -> p (a b)")
                    .rearrange("p (a b) -> p a b", b=P))
            pl.stage("sync", s_xbar, waits=xw, inc="dXB", dma=True,
                     tag=f"{K}_xbar_{c4}")

            for d8 in range(8):
                tw = [f"{K}_xbar_{c4}", "wsum_bc_mm"]
                if d8 == 0:
                    if c4 >= 1:
                        tw.append(f"{K}_xdev_{c4-1}")
                    elif k > 0:
                        tw.append(f"{pk}_xdev_3")
                    else:
                        tw.append("wv_bc_mm")

                def t_xdmm(t, c4=c4, d8=d8):
                    return t.matmul(pst0[0:E, :], wxt_b[:, d8, :], xbuf[:, :, d8, :],
                                    start=(d8 == 0), stop=(d8 == 7))
                pl.stage("tensor", t_xdmm, waits=tw, inc="sT", tag=f"{K}_xdmm_{c4}_{d8}")

            vw = [f"{K}_xdmm_{c4}_7"]
            if k > 0 and c4 == 0:
                vw.append(f"{pk}_bctr_done")

            def v_xdev(v, c4=c4):
                return v.tensor_copy(out=xdbl_fm[:, c4 * CH:(c4 + 1) * CH],
                                     in_=pst0[0:E, :])
            pl.stage("vector", v_xdev, waits=vw, inc="sV", tag=f"{K}_xdev_{c4}")

        def v_dtraw(v):
            return v.tensor_copy(out=dtraw_fm[:, :], in_=xdbl_fm[0:RK, :])
        pl.stage("vector", v_dtraw, waits=[f"{K}_xdev_3"], inc="sV", tag=f"{K}_dtraw")

        for ti in range(LT):
            bw = [f"{K}_xdev_{ti // 4}", "wl_plain"]
            if ti >= 1:
                bw.append(f"{K}_bcfix_{ti-1}")
            elif k > 0:
                bw.append(f"{pk}_bcfix_{LT-1}")

            def t_bctr(t, ti=ti):
                return t.transpose(pst1[:, 0:32],
                                   xdbl_fm[RK:E, ti * P:(ti + 1) * P],
                                   identf[RK:E, RK:E])
            pl.stage("tensor", t_bctr, waits=bw, inc="sT", tag=f"{K}_bctr_{ti}")

            fw = [f"{K}_bctr_{ti}", f"{K}_nmr_{ti}", "wsum_bc", "wv_bc"]
            if k > 0 and ti < 1:
                fw.append(f"{pk}_ccast_done")

            def v_bcfix(v, ti=ti):
                v.tensor_scalar(out=xdbl_tm[:, ti, :], in0=pst1[:, 0:32],
                                scalar1=rcol[:, ti:ti + 1], scalar2=None, op0=OP.mult)
                i2 = v.scalar_tensor_tensor(out=xdbl_tm[:, ti, :], in0=wsum96[:, RK:E],
                                            scalar=nmrcol[:, ti:ti + 1],
                                            in1=xdbl_tm[:, ti, :], op0=OP.mult, op1=OP.add)
                fence(v, i2)
                return v.tensor_scalar(out=ccast[:, ti, :], in0=xdbl_tm[:, ti, :],
                                       scalar1=1.0, scalar2=None, op0=OP.mult)
            pl.stage("vector", v_bcfix, waits=fw, inc="sV", tag=f"{K}_bcfix_{ti}")
        pl.alias(f"{K}_bctr_done", f"{K}_bctr_{LT-1}")

        for tp in range(0, LT, 2):
            dw = [f"{K}_dtraw", "wl_cast"]
            if tp >= 2:
                dw.append(f"{K}_dtfix1_{tp-2}")
            elif k > 0:
                dw.append(f"{pk}_dtfix1_{LT-2}")

            def t_dtmm(t, tp=tp):
                t.matmul(pst0[:, 0:DQ], dtraw_fm[:, tp * P:(tp + 1) * P],
                         wdt_b[:, :], start=True, stop=True)
                return t.matmul(pst0[:, DQ:2 * DQ],
                                dtraw_fm[:, (tp + 1) * P:(tp + 2) * P],
                                wdt_b[:, :], start=True, stop=True)
            pl.stage("tensor", t_dtmm, waits=dw, inc="sT", tag=f"{K}_dtmm_{tp}")

            tw0 = [f"{K}_dtmm_{tp}"]
            if k > 0 and tp < 2:
                tw0.append(f"{pk}_dttr_done")
                tw0.append(f"{pk}_a2a_wr")

            def v_dtfix1(v, tp=tp):
                for j in range(2):
                    ins = v.tensor_scalar(out=tmbuf[:, tp + j, :],
                                          in0=pst0[:, j * DQ:(j + 1) * DQ],
                                          scalar1=rcol[:, tp + j:tp + j + 1],
                                          scalar2=None, op0=OP.mult)
                return ins
            pl.stage("vector", v_dtfix1, waits=tw0, inc="sV", tag=f"{K}_dtfix1_{tp}")

            def v_dtfix2(v, tp=tp):
                for j in range(2):
                    v.scalar_tensor_tensor(out=tmbuf[:, tp + j, :], in0=wvbc[:, :],
                                           scalar=nmrcol[:, tp + j:tp + j + 1],
                                           in1=tmbuf[:, tp + j, :],
                                           op0=OP.mult, op1=OP.add)
                    ins = v.tensor_tensor(out=tmbuf[:, tp + j, :],
                                          in0=tmbuf[:, tp + j, :],
                                          in1=bdtbc_s[:, :], op=OP.add)
                return ins
            pl.stage("vector", v_dtfix2, waits=[], inc="sV", tag=f"{K}_dtfix2_{tp}")

            spw = [f"{K}_dtfix2_{tp}"]
            if tp >= 2:
                spw.append(f"{K}_dtsp_{tp-2}")

            def s_sp(s, tp=tp):
                s.activation(out=spbuf[:, :, :], in_=tmbuf[:, tp:tp + 2, :],
                             func=AF.Abs)
                s.activation(out=spbuf[:, :, :], in_=spbuf[:, :, :],
                             func=AF.Exp, scale=consts[:, 2:3])
                return s.activation(out=spbuf[:, :, :], in_=spbuf[:, :, :],
                                    func=AF.Ln, bias=consts[:, 1:2])
            pl.stage("scalar", s_sp, waits=spw, inc="sS", tag=f"{K}_sp_{tp}")

            def v_dtsp(v, tp=tp):
                return v.scalar_tensor_tensor(out=tmbuf[:, tp:tp + 2, :],
                                              in0=tmbuf[:, tp:tp + 2, :],
                                              scalar=0.0, in1=spbuf[:, :, :],
                                              op0=OP.max, op1=OP.add)
            pl.stage("vector", v_dtsp, waits=[f"{K}_sp_{tp}"], inc="sV",
                     tag=f"{K}_dtsp_{tp}")

            def v_dtu(v, tp=tp):
                return v.tensor_tensor(out=dtu[:, tp:tp + 2, :],
                                       in0=dtu[:, tp:tp + 2, :],
                                       in1=tmbuf[:, tp:tp + 2, :], op=OP.mult)
            pl.stage("vector", v_dtu, waits=[f"{K}_ln1_{tp+1}", f"{K}_dtsp_{tp}"],
                     inc="sV", tag=f"{K}_dtu_{tp+1}")

            for ti in (tp, tp + 1):
              for db2 in range(2):
                trw = [f"{K}_dtsp_{tp}"]
                if ti >= 1 or db2 == 1:
                    prev = f"{K}_dtev_{ti}_{db2-1}" if db2 == 1 else f"{K}_dtev_{ti-1}_1"
                    trw.append(prev)
                elif k > 0:
                    trw.append(f"{pk}_dtev_{LT-1}_1")

                def t_dttr(t, ti=ti, db2=db2):
                    return t.transpose(pst1[:, 0:P], tmbuf[:, ti, db2 * P:(db2 + 1) * P],
                                       identf[:, :])
                pl.stage("tensor", t_dttr, waits=trw, inc="sT", tag=f"{K}_dttr_{ti}_{db2}")

                ew = [f"{K}_dttr_{ti}_{db2}"]
                if k > 0 and ti == 0 and db2 == 0:
                    ew.append(f"{pk}_abdone")

                def s_dtev(s, ti=ti, db2=db2):
                    return s.activation(out=dt_fm[:, db2, ti * P:(ti + 1) * P],
                                        in_=pst1[:, 0:P], func=AF.Copy)
                pl.stage("scalar", s_dtev, waits=ew, inc="sS", tag=f"{K}_dtev_{ti}_{db2}")
        pl.alias(f"{K}_dttr_done", f"{K}_dttr_{LT-1}_1")

        if dbg:
            def s_dbg_ln1e(s, k=k):
                return s.dma_start(out=d_ln1e.ap()[k].rearrange("(a p) c -> p a c", p=P),
                                   in_=ln1d[:, :, :])
            pl.stage("sync", s_dbg_ln1e, waits=[f"{K}_ln1_{LT-1}"], inc="dSP",
                     dma=True, tag=f"{K}_dbg_ln1e")

            def s_dbg_m2(s, k=k):
                return [s.dma_start(out=d_xdbl.ap()[k].rearrange("(a p) c -> p a c", p=P),
                                    in_=xdbl_tm[:, :, :]),
                        s.dma_start(out=d_dt.ap()[k].rearrange("(a p) c -> p a c", p=P),
                                    in_=tmbuf[:, :, :]),
                        s.dma_start(out=d_dtu.ap()[k].rearrange("(a p) c -> p a c", p=P),
                                    in_=dtu[:, :, :]),
                        s.dma_start(out=d_ln1d.ap()[k].rearrange("(a p) c -> p a c", p=P),
                                    in_=ln1d[:, :, :]),
                        s.dma_start(out=d_ccast.ap()[k].rearrange("(a p) c -> p a c", p=P),
                                    in_=ccast[:, :, :]),
                        s.dma_start(out=d_stats.ap()[k, :, 0:16], in_=mcol[:, :]),
                        s.dma_start(out=d_stats.ap()[k, :, 16:32], in_=rcol[:, :]),
                        s.dma_start(out=d_stats.ap()[k, :, 32:48], in_=nmrcol[:, :])]
            s_dbg_m2.dma_n = 8
            pl.stage("sync", s_dbg_m2,
                     waits=[f"{K}_dtsp_{LT-2}", f"{K}_dtu_{LT-1}", f"{K}_bcfix_{LT-1}",
                            f"{K}_ln1_{LT-1}"],
                     inc="dSP", dma=True, tag=f"{K}_dbg_m2")

        for ti in range(LT):
            cw = [f"{K}_bcfix_{ti}", "v_setup"]
            if ti >= 1:
                cw.append(f"{K}_crow_{ti-1}")
            elif k > 0:
                cw.append(f"{pk}_crow_{LT-1}")

            def t_ctr(t, ti=ti):
                return t.transpose(pdb0[0:2 * NST, 0:P], ccast[:, ti, :], identb[:, :])
            pl.stage("tensor", t_ctr, waits=cw, inc="sT", tag=f"{K}_ctr_{ti}")

            crw = [f"{K}_ctr_{ti}"]
            if k > 0 and ti < 1:
                crw.append(f"{pk}_crow_hbm")

            def v_crow(v, ti=ti):
                return v.tensor_copy(out=crows[:, ti, :], in_=pdb0[0:2 * NST, 0:P])
            pl.stage("vector", v_crow, waits=crw, inc="sV", tag=f"{K}_crow_{ti}")

        pl.alias(f"{K}_ccast_done", f"{K}_ctr_{LT-1}")

        def s_crow_hbm(s, k=k):
            return s.dma_start(out=crows_hbm.ap()[k, :, :],
                               in_=crows[:, :, :].rearrange("n t p -> n (t p)"))
        pl.stage("sync", s_crow_hbm, waits=[f"{K}_crow_{LT-1}"], inc="dCR", dma=True,
                 tag=f"{K}_crow_hbm")

        ftw = [f"{K}_dtu_{LT-1}"]
        if k > 0:
            ftw.append(f"{pk}_scan_1_{NST-1}_1")

        def s_dtuT(s):
            return s.dma_start_transpose(
                out=dtuT[:, :, :, :].rearrange("p a b c -> p (a b c)")
                .rearrange("p (a b) -> p a b", b=P),
                in_=dtu[:, :, :].rearrange("p a b -> p (a b)")
                .rearrange("p (a b) -> p a b", b=P))
        pl.stage("sync", s_dtuT, waits=ftw, inc="dFT", dma=True, tag=f"{K}_dtuT")

        for H in range(2):
            for n in range(NST):
                sl = n % 2
                hn = H * NST + n

                cbw = [f"{K}_crow_hbm"]
                if hn >= 2:
                    cbw.append(f"{K}_scan_{H}_{n-2}_1" if n >= 2
                               else f"{K}_scan_{H-1}_{NST-2+n}_1")
                elif k > 0:
                    cbw.append(f"{pk}_scan_1_{NST-2+hn}_1")

                def s_cbc(s, k=k, H=H, n=n, sl=sl):
                    return s.dma_start(
                        out=cbc[:, sl, :],
                        in_=crows_hbm.ap()[k, NST + n:NST + n + 1,
                                           H * 1024:(H + 1) * 1024]
                        .broadcast_to((P, 1024)))
                pl.stage("gpsimd", s_cbc, waits=cbw, inc=f"dCB{n % 2}", dma=True,
                         tag=f"{K}_cbc_{H}_{n}")

                def s_bbc(s, k=k, H=H, n=n, sl=sl):
                    return s.dma_start(
                        out=bbc[:, sl, :],
                        in_=crows_hbm.ap()[k, n:n + 1, H * 1024:(H + 1) * 1024]
                        .broadcast_to((P, 1024)))
                pl.stage("gpsimd", s_bbc, waits=list(cbw), inc=f"dBB{n % 2}", dma=True,
                         tag=f"{K}_bbc_{H}_{n}")

                for db2 in range(2):
                    idx = db2 * NST + n
                    gi = hn * 2 + db2
                    asl = gi % 2

                    aw = [f"{K}_dtev_{LT-1}_1", "acols_neg"]
                    if gi >= 2:
                        aw.append(f"{K}_scan_{H}_{n-1}_{db2}" if n >= 1
                                  else f"{K}_scan_{H-1}_{NST-1}_{db2}")
                    elif k > 0:
                        aw.append(f"{pk}_scan_1_{NST-1}_{db2}")

                    def s_ab(s, H=H, db2=db2, n=n, asl=asl):
                        return s.activation(out=absl[:, asl, :],
                                            in_=dt_fm[:, db2, H * 1024:(H + 1) * 1024],
                                            func=AF.Exp,
                                            scale=acols[:, db2 * NST + n:db2 * NST + n + 1])
                    pl.stage("scalar", s_ab, waits=aw, inc="sS",
                             tag=f"{K}_ab_{H}_{n}_{db2}")

                    sw = [f"{K}_ab_{H}_{n}_{db2}", f"{K}_dtuT",
                          f"{K}_cbc_{H}_{n}", f"{K}_bbc_{H}_{n}"]
                    if gi >= 2:
                        sw.append(f"{K}_ymm_{H}_{n-1}_{db2}" if n >= 1
                                  else f"{K}_ymm_{H-1}_{NST-1}_{db2}")
                    elif k > 0:
                        sw.append(f"{pk}_ymm_1_{NST-1}_{db2}")

                    def v_scan(v, H=H, db2=db2, n=n, sl=sl, asl=asl, idx=idx):
                        v.tensor_tensor(
                            out=dbT[:, :].rearrange("p (a b) -> p a b", b=P),
                            in0=dtuT[:, H * 8:(H + 1) * 8, db2, :],
                            in1=bbc[:, sl, :].rearrange("p (a b) -> p a b", b=P),
                            op=OP.mult)
                        v.tensor_tensor_scan(out=hsl[:, :], data0=absl[:, asl, :],
                                             data1=dbT[:, :],
                                             initial=statec[:, idx:idx + 1],
                                             op0=OP.mult, op1=OP.add)
                        v.tensor_tensor(out=tsl[:, asl, :], in0=hsl[:, :],
                                        in1=cbc[:, sl, :], op=OP.mult)
                        return v.tensor_copy(out=statec[:, idx:idx + 1],
                                             in_=hsl[:, 1023:1024])
                    pl.stage("vector", v_scan, waits=sw, inc="sV",
                             tag=f"{K}_scan_{H}_{n}_{db2}")

                    yw = [f"{K}_scan_{H}_{n}_{db2}"]
                    if n == 0:
                        if H > 0:
                            yw.append(f"{K}_yev_{H-1}_{db2}_1")
                        elif k > 0:
                            yw.append(f"{pk}_yev_1_{db2}_1")

                    def t_ymm(t, H=H, db2=db2, n=n, asl=asl):
                        t.matmul(py[db2 * 2 + 0][:, :], identb[:, :], tsl[:, asl, 0:CH],
                                 start=(n == 0), stop=(n == NST - 1))
                        return t.matmul(py[db2 * 2 + 1][:, :], identb[:, :],
                                        tsl[:, asl, CH:1024],
                                        start=(n == 0), stop=(n == NST - 1))
                    pl.stage("tensor", t_ymm, waits=yw, inc="sT",
                             tag=f"{K}_ymm_{H}_{n}_{db2}")

            for db2 in range(2):
                for c01 in range(2):
                    yw2 = [f"{K}_ymm_{H}_{NST-1}_{db2}"]
                    prev_ev = None
                    if c01 == 1:
                        prev_ev = (H, db2, 0)
                    elif db2 == 1:
                        prev_ev = (H, 0, 1)
                    if prev_ev is not None:
                        yw2.append(f"{K}_ytr_{prev_ev[0]}_{prev_ev[1]}_{prev_ev[2]}_3")

                    def s_yev_c(s, db2=db2, c01=c01):
                        return s.activation(out=ybuf[:, :], in_=py[db2 * 2 + c01][:, :],
                                            func=AF.Copy)
                    pl.stage("scalar", s_yev_c, waits=yw2, inc="sS",
                             tag=f"{K}_yev_{H}_{db2}_{c01}")

                    for lt4 in range(4):
                        ti = H * 8 + c01 * 4 + lt4
                        trw = [f"{K}_yev_{H}_{db2}_{c01}"]
                        seq = []
                        for hh in range(2):
                            for dd in range(2):
                                for cc in range(2):
                                    for ll in range(4):
                                        seq.append((hh, dd, cc, ll))
                        cur = seq.index((H, db2, c01, lt4))
                        if cur > 1:
                            ph, pd, pc, plt = seq[cur - 2]
                            trw.append(f"{K}_attn_{ph}_{pd}_{pc}_{plt}")
                        elif k > 0:
                            trw.append(f"{pk}_attn_1_1_1_{2+cur}")

                        pty = pst1 if cur % 2 == 0 else pst0

                        def t_ytr(t, lt4=lt4, pty=pty):
                            return t.transpose(pty[:, 0:P],
                                               ybuf[:, lt4 * P:(lt4 + 1) * P],
                                               identf[:, :])
                        pl.stage("tensor", t_ytr, waits=trw, inc="sT",
                                 tag=f"{K}_ytr_{H}_{db2}_{c01}_{lt4}")

                        atw = [f"{K}_ytr_{H}_{db2}_{c01}_{lt4}", f"{K}_ln1_{ti}",
                               f"{K}_dttr_done", f"{K}_sp_{(ti // 2) * 2}"] \
                            + ([f"{K}_dbg_m2"] if dbg else [])
                        if k > 0 and cur == 0:
                            atw.append(f"{pk}_a2a_wr")

                        def v_attn(v, ti=ti, db2=db2, pty=pty):
                            return v.tensor_tensor(out=tmbuf[:, ti, db2 * P:(db2 + 1) * P],
                                                   in0=ln1d[:, ti, db2 * P:(db2 + 1) * P],
                                                   in1=pty[:, 0:P], op=OP.add)
                        pl.stage("vector", v_attn, waits=atw, inc="sV",
                                 tag=f"{K}_attn_{H}_{db2}_{c01}_{lt4}")

        pl.alias(f"{K}_scan_done", f"{K}_scan_1_{NST-1}_1")
        pl.alias(f"{K}_abdone", f"{K}_ab_1_{NST-1}_1")
        pl.alias(f"{K}_attn_done", f"{K}_attn_1_1_1_3")


        @dma_n(8)
        def s_a2a_wr(s, k=k):
            ins = []
            for j in range(8):
                st4 = j % 4
                ins.append(s.dma_start(
                    out=a2a_in[k].ap()[j * CH:(j + 1) * CH, :]
                    .rearrange("(a p) c -> p a c", p=P),
                    in_=tmbuf[:, 4 * st4:4 * st4 + 4, :]))
            return ins
        pl.stage("gpsimd", s_a2a_wr, waits=[f"{K}_attn_done"], inc="dAW", dma=True,
                 tag=f"{K}_a2a_wr")

        if dbg:
            def s_dbg_attn(s, k=k):
                return s.dma_start(out=d_attn.ap()[k].rearrange("(a p) c -> p a c", p=P),
                                   in_=tmbuf[:, :, :])
            pl.stage("sync", s_dbg_attn, waits=[f"{K}_attn_done"], inc="dSP", dma=True,
                     tag=f"{K}_dbg_attn")

        def g_a2a(g, k=k):
            g.wait_ge_ext = None
            with allow_ici_all_to_all():
                return g.collective_compute(
                    "AllToAll", OP.bypass, replica_groups=[list(range(8))],
                    ins=[a2a_in[k].ap().opt()], outs=[a2a_out[k].ap().opt()])
        pl.stage("gpsimd", g_a2a, waits=[f"{K}_a2a_wr"], inc="sCC", tag=f"{K}_a2a")

    mixer_done = [pl.last["vector"], pl.last["scalar"], pl.last["tensor"],
                  pl.last["gpsimd"]]
    gf_first = [True]
    for k in range(NBLK):
        K = f"A{k}"
        kw_base = [f"k{k}_a2a"] + ([f"A{k-1}_tfsp"] if k else mixer_done)
        for ti in range(4):
            @dma_n(2)
            def s_fxl(s, k=k, ti=ti):
                i1 = s.dma_start(out=fx2[:, ti, :].rearrange("p (a b) -> p a b", b=DQ),
                                 in_=a2a_out[k].ap()
                                 .rearrange("(s a p) c -> p a s c", a=4, p=P)[:, ti, 0:4, :])
                i2 = s.dma_start(out=fln2[:, ti, :].rearrange("p (a b) -> p a b", b=DQ),
                                 in_=a2a_out[k].ap()
                                 .rearrange("(s a p) c -> p a s c", a=4, p=P)[:, ti, 4:8, :])
                return [i1, i2]
            if k == 0:
                lw = list(kw_base)
            else:
                lw = [f"k{k}_a2a", f"A{k-1}_x2sp_{ti}", f"A{k-1}_ln2sp_{ti}"]
            pl.stage("gpsimd", s_fxl, waits=lw, inc=f"dFL{ti}", dma=True,
                     tag=f"{K}_ld_{ti}")

        def emit_ul(ti, k=k, K=K):
            def s_ul(s, k=k, ti=ti):
                return s.dma_start(out=u_tm[:, ti, :],
                                   in_=u_tok[k, ti * P:(ti + 1) * P, :])
            if k == 0:
                uw = list(kw_base)
            else:
                uw = [f"A{k-1}_x2_{ti}"]
            pl.stage("sync", s_ul, waits=uw, inc=f"dUS{ti}", dma=True,
                     tag=f"{K}_ul_{ti}")
        for uti in range(4):
            emit_ul(uti)

        def v_asm4(v):
            ins = None
            for ti in range(4):
                v.tensor_scalar(out=fx2[:, ti, :], in0=fx2[:, ti, :],
                                scalar1=masks[:, 0:1], scalar2=None, op0=OP.mult)
                ins = v.scalar_tensor_tensor(out=fx2[:, ti, :], in0=fln2[:, ti, :],
                                             scalar=masks[:, 1:2], in1=fx2[:, ti, :],
                                             op0=OP.mult, op1=OP.add)
            return ins
        pl.stage("vector", v_asm4, waits=[f"{K}_ld_{t2}" for t2 in range(4)],
                 inc="sV", tag=f"{K}_asm")

        ssw = [f"{K}_asm"]
        if k > 0:
            ssw += [f"A{k-1}_fxbar", f"A{k-1}_gf_3", f"A{k-1}_ln2c"]

        def s_sum4(s):
            ins = None
            for ti in range(4):
                ins = s.activation(out=fbf[:, ti, :], in_=fx2[:, ti, :],
                                   func=AF.Copy, accum_out=m2col[:, ti:ti + 1])
            return ins
        pl.stage("scalar", s_sum4, waits=ssw, inc="sS", tag=f"{K}_ssum")

        def s_sq4(s):
            ins = None
            for ti in range(4):
                ins = s.activation(out=fbf[:, ti, :], in_=fx2[:, ti, :],
                                   func=AF.Square, accum_out=r2col[:, ti:ti + 1])
            return ins
        pl.stage("scalar", s_sq4, waits=[], inc="sS", tag=f"{K}_ssq")

        def v_stat4(v):
            v.tensor_scalar(out=m2col[:, 0:4], in0=m2col[:, 0:4],
                            scalar1=1.0 / D, scalar2=None, op0=OP.mult)
            i2r = v.tensor_scalar(out=r2col[:, 0:4], in0=r2col[:, 0:4],
                                  scalar1=1.0 / D, scalar2=None, op0=OP.mult)
            fence(v, i2r)
            ins = None
            for ti in range(4):
                ins = v.scalar_tensor_tensor(out=r2col[:, ti:ti + 1],
                                             in0=m2col[:, ti:ti + 1],
                                             scalar=m2col[:, ti:ti + 1],
                                             in1=r2col[:, ti:ti + 1],
                                             op0=OP.mult, op1=OP.subtract)
            return ins
        pl.stage("vector", v_stat4, waits=[f"{K}_ssum", f"{K}_ssq"], inc="sV",
                 tag=f"{K}_s3")

        def s_rstd4(s):
            s.activation(out=r2col[:, 0:4], in_=r2col[:, 0:4],
                         func=AF.Ln, scale=consts[:, 2:3], bias=consts[:, 0:1])
            return s.activation(out=r2col[:, 0:4], in_=r2col[:, 0:4],
                                func=AF.Exp, scale=consts[:, 3:4])
        pl.stage("scalar", s_rstd4, waits=[f"{K}_s3"], inc="sS", tag=f"{K}_s4")

        def v_nmr4(v):
            ins = None
            for ti in range(4):
                ins = v.scalar_tensor_tensor(out=nm2col[:, ti:ti + 1],
                                             in0=m2col[:, ti:ti + 1], scalar=-1.0,
                                             in1=r2col[:, ti:ti + 1],
                                             op0=OP.mult, op1=OP.mult)
            return ins
        pl.stage("vector", v_nmr4, waits=[f"{K}_s4"], inc="sV", tag=f"{K}_s5")

        for ti in range(4):
            def v_x2(v, ti=ti):
                v.tensor_scalar(out=fx2[:, ti, :], in0=fx2[:, ti, :],
                                scalar1=r2col[:, ti:ti + 1], scalar2=nm2col[:, ti:ti + 1],
                                op0=OP.mult, op1=OP.add)
                return v.tensor_tensor(out=fx2[:, ti, :], in0=fx2[:, ti, :],
                                       in1=u_tm[:, ti, :], op=OP.add)
            pl.stage("vector", v_x2, waits=[f"{K}_s5", f"{K}_ul_{ti}"],
                     inc="sV", tag=f"{K}_x2_{ti}")

            def s_x2sp(s, k=k, ti=ti):
                return s.dma_start(out=x2_sp.ap()[k, ti * P:(ti + 1) * P, :],
                                   in_=fx2[:, ti, :])
            pl.stage("sync", s_x2sp, waits=[f"{K}_x2_{ti}"], inc=f"dXS{k}", dma=True,
                     tag=f"{K}_x2sp_{ti}")

        def s_tsum4(s):
            ins = None
            for ti in range(4):
                ins = s.activation(out=fbf[:, ti, :], in_=fx2[:, ti, :],
                                   func=AF.Copy, accum_out=m2col[:, ti:ti + 1])
            return ins
        pl.stage("scalar", s_tsum4, waits=[f"{K}_x2_3"], inc="sS", tag=f"{K}_tsum")

        def s_tsq4(s):
            ins = None
            for ti in range(4):
                ins = s.activation(out=fbf[:, ti, :], in_=fx2[:, ti, :],
                                   func=AF.Square, accum_out=r2col[:, ti:ti + 1])
            return ins
        pl.stage("scalar", s_tsq4, waits=[], inc="sS", tag=f"{K}_tsq")

        def v_tstat4(v):
            v.tensor_scalar(out=m2col[:, 0:4], in0=m2col[:, 0:4],
                            scalar1=1.0 / D, scalar2=None, op0=OP.mult)
            i2r = v.tensor_scalar(out=r2col[:, 0:4], in0=r2col[:, 0:4],
                                  scalar1=1.0 / D, scalar2=None, op0=OP.mult)
            fence(v, i2r)
            ins = None
            for ti in range(4):
                ins = v.scalar_tensor_tensor(out=r2col[:, ti:ti + 1],
                                             in0=m2col[:, ti:ti + 1],
                                             scalar=m2col[:, ti:ti + 1],
                                             in1=r2col[:, ti:ti + 1],
                                             op0=OP.mult, op1=OP.subtract)
            return ins
        pl.stage("vector", v_tstat4, waits=[f"{K}_tsum", f"{K}_tsq"], inc="sV",
                 tag=f"{K}_t3")

        def s_trstd4(s):
            s.activation(out=r2col[:, 0:4], in_=r2col[:, 0:4],
                         func=AF.Ln, scale=consts[:, 2:3], bias=consts[:, 0:1])
            return s.activation(out=r2col[:, 0:4], in_=r2col[:, 0:4],
                                func=AF.Exp, scale=consts[:, 3:4])
        pl.stage("scalar", s_trstd4, waits=[f"{K}_t3"], inc="sS", tag=f"{K}_t4")

        def v_t5b(v):
            i2 = None
            for ti in range(4):
                i2 = v.scalar_tensor_tensor(out=nm2col[:, ti:ti + 1],
                                            in0=m2col[:, ti:ti + 1], scalar=-1.0,
                                            in1=r2col[:, ti:ti + 1],
                                            op0=OP.mult, op1=OP.mult)
            fence(v, i2)
            ins = None
            for ti in range(4):
                v.tensor_scalar(out=fln2[:, ti, :], in0=fx2[:, ti, :],
                                scalar1=r2col[:, ti:ti + 1], scalar2=nm2col[:, ti:ti + 1],
                                op0=OP.mult, op1=OP.add)
                ins = v.tensor_copy(out=fbf[:, ti, :], in_=fln2[:, ti, :])
            return ins
        pl.stage("vector", v_t5b, waits=[f"{K}_t4"], inc="sV", tag=f"{K}_ln2c")

        for ti in range(4):
            def s_ln2sp(s, k=k, ti=ti):
                return s.dma_start(out=ln2_sp.ap()[k, ti * P:(ti + 1) * P, :],
                                   in_=fln2[:, ti, :])
            pl.stage("sync", s_ln2sp, waits=[f"{K}_ln2c"], inc=f"dLS{k}", dma=True,
                     tag=f"{K}_ln2sp_{ti}")

            gfw = [f"{K}_ln2c"]
            if not gf_first[0]:
                pass
            def t_gf(t, ti=ti, first=(gf_first[0] and ti == 0), k=k):
                st = (k == 0 and ti == 0)
                sp = (k == NBLK - 1 and ti == 3)
                t.matmul(py[0][0:1, :], ones_col_b[:, 0:1], fbf[:, ti, 0:CH],
                         start=st, stop=sp)
                return t.matmul(py[1][0:1, :], ones_col_b[:, 0:1], fbf[:, ti, CH:D],
                                start=st, stop=sp)
            pl.stage("tensor", t_gf, waits=gfw, inc="sT", tag=f"{K}_gf_{ti}")
        gf_first[0] = False

        xw = [f"{K}_ln2c"]
        if k > 0:
            xw.append(f"A{k-1}_h1_mm_7")

        def s_fxbar(s):
            return s.dma_start_transpose(
                out=ffm[:, :, :, :].rearrange("p a b c -> p (a b c)")
                .rearrange("p (a b) -> p a b", b=P),
                in_=fbf[:, :, :].rearrange("p a b -> p (a b)")
                .rearrange("p (a b) -> p a b", b=P))
        pl.stage("sync", s_fxbar, waits=xw, inc="dFX", dma=True, tag=f"{K}_fxbar")

        def ffn_mms(K, src_fm, wsrc, nk, nh, outbuf, bcol, psA, psB, wtag, wait_extra):
            for hb in range(nh):
                mw = [wtag] + (wait_extra if hb == 0 else [])
                if hb >= 2:
                    mw.append(f"{K}_{outbuf}_ge_{hb-2}")

                def t_mm(t, hb=hb, src_fm=src_fm, wsrc=wsrc, nk=nk, psA=psA, psB=psB):
                    ps = psA if hb % 2 == 0 else psB
                    ins = None
                    for kb in range(nk):
                        ins = t.matmul(ps[:, :], wsrc[:, kb, hb, :], src_fm(kb),
                                       start=(kb == 0), stop=(kb == nk - 1))
                    return ins
                pl.stage("tensor", t_mm, waits=mw, inc="sT", tag=f"{K}_{outbuf}_mm_{hb}")

                def s_ge(s, hb=hb, outbuf=outbuf, bcol=bcol, psA=psA, psB=psB):
                    ps = psA if hb % 2 == 0 else psB
                    ob = {"h1": fh1, "h2": fh2, "h3": fh3}[outbuf]
                    return s.activation(out=ob[:, hb, :], in_=ps[:, :], func=AF.Gelu,
                                        bias=bcol[:, hb:hb + 1])
                pl.stage("scalar", s_ge, waits=[f"{K}_{outbuf}_mm_{hb}"], inc="sS",
                         tag=f"{K}_{outbuf}_ge_{hb}")

        we = ["wl_cast"] + ([f"A{k-1}_tfsp"] if k > 0 else [])
        ffn_mms(K, lambda kb: ffm[:, :, kb, :], w1b, 8, 8, "h1", b1c, pst0, pst1,
                f"{K}_fxbar", we)
        ffn_mms(K, lambda kb: fh1[:, kb, :], w2b, 8, 4, "h2", b2c, pst0, pst1,
                f"{K}_h1_ge_7", [])
        ffn_mms(K, lambda kb: fh2[:, kb, :], w3b, 4, 2, "h3", b3c, pst0, pst1,
                f"{K}_h2_ge_3", [])

        def t_tf(t):
            t.matmul(pst0[0:1, :], w4b[:, 0:1], fh3[:, 0, :], start=True, stop=False)
            return t.matmul(pst0[0:1, :], w4b[:, 1:2], fh3[:, 1, :],
                            start=False, stop=True)
        pl.stage("tensor", t_tf, waits=[f"{K}_h3_ge_1", f"{K}_h1_mm_7" if False else f"{K}_h3_ge_1"],
                 inc="sT", tag=f"{K}_tfmm")

        def s_tf(s):
            return s.activation(out=frow[0:1, :], in_=pst0[0:1, :], func=AF.Tanh,
                                scale=consts[0:1, 4:5], bias=bias4[0:1, 0:1])
        pl.stage("scalar", s_tf, waits=[f"{K}_tfmm", "v_halfb4"], inc="sS",
                 tag=f"{K}_tfrow")

        def s_tfsp(s, k=k):
            return s.dma_start(out=tf0_sp.ap()[k:k + 1, :], in_=frow[0:1, :])
        pl.stage("sync", s_tfsp, waits=[f"{K}_tfrow"], inc=f"dTS{k}", dma=True,
                 tag=f"{K}_tfsp")

    def v_gfev(v):
        v.tensor_copy(out=gmr[0:1, 0:CH], in_=py[0][0:1, :])
        v.tensor_copy(out=gmr[0:1, CH:D], in_=py[1][0:1, :])
        return v.tensor_scalar(out=gmr[0:1, :], in0=gmr[0:1, :], scalar1=1.0 / L,
                               scalar2=None, op0=OP.mult)
    pl.stage("vector", v_gfev, waits=[f"A{NBLK-1}_gf_3"], inc="sV", tag="gfev")

    def s_gfar_wr(s):
        return s.dma_start(out=gfar_in.ap()[0:1, :], in_=gmr[0:1, :])
    pl.stage("sync", s_gfar_wr, waits=["gfev"], inc="dGW", dma=True, tag="gfar_wr")

    def g_ar(g):
        return g.collective_compute("AllReduce", OP.add,
                                    replica_groups=[[0, 1, 2, 3], [4, 5, 6, 7]],
                                    ins=[gfar_in.ap().opt()], outs=[gfar_out.ap().opt()])
    pl.stage("gpsimd", g_ar, waits=["gfar_wr", f"k{NBLK-1}_a2a"], inc="sCC", tag="gf_ar")

    def s_gfar_rd(s):
        return s.dma_start(out=gmr[0:1, :], in_=gfar_out.ap()[0:1, :])
    pl.stage("sync", s_gfar_rd, waits=["gf_ar"], inc="dGR", dma=True, tag="gfar_rd")

    if dbg:
        def s_dbg_gm(s):
            return [s.dma_start(out=d_gm.ap()[0:1, :], in_=gfar_in.ap()[0:1, :]),
                    s.dma_start(out=d_gm.ap()[1:2, :], in_=gfar_out.ap()[0:1, :])]
        s_dbg_gm.dma_n = 2
        pl.stage("sync", s_dbg_gm, waits=["gfar_rd"], inc="dSP", dma=True, tag="dbg_gm")

    def g_gcw(g):
        i1 = g.dma_start(out=gw1b[:, :, :, :], in_=gc_w1)
        return [i1, g.dma_start(out=gw2b[:, :, :, :], in_=gc_w2)]
    g_gcw.dma_n = 2
    pl.stage("gpsimd", g_gcw, waits=[f"k{NBLK-1}_a2a", f"A{NBLK-1}_h1_ge_7"],
             inc="dG", dma=True, tag="gcw_ld")

    def t_gmtr(t):
        ins = None
        for j in range(8):
            ins = t.transpose(pst1[:, j:j + 1],
                              gmr[0:1, j * P:(j + 1) * P], identf[0:1, 0:1])
        return ins
    pl.stage("tensor", t_gmtr, waits=["gfar_rd", f"A{NBLK-1}_tfsp"], inc="sT",
             tag="gmtr_7")

    def v_gmc(v):
        return v.tensor_copy(out=gmc[:, 0:8], in_=pst1[:, 0:8])
    pl.stage("vector", v_gmc, waits=["gmtr_7"], inc="sV", tag="gmc_7")

    for hb in range(4):
        def t_g1(t, hb=hb):
            ins = None
            for kb in range(8):
                ins = t.matmul(pst0[:, hb:hb + 1], gw1b[:, kb, hb, :], gmc[:, kb:kb + 1],
                               start=(kb == 0), stop=(kb == 7))
            return ins
        pl.stage("tensor", t_g1, waits=["gmc_7", "gcw_ld"], inc="sT", tag=f"g1_{hb}")
    for hb in range(4):
        def s_g1c(s, hb=hb):
            return s.activation(out=gfh1[:, hb:hb + 1], in_=pst0[:, hb:hb + 1],
                                func=AF.Gelu, bias=gb1c[:, hb:hb + 1])
        pl.stage("scalar", s_g1c, waits=["g1_3"], inc="sS", tag=f"g1c_{hb}")

    for hb in range(2):
        def t_g2(t, hb=hb):
            ins = None
            for kb in range(4):
                ins = t.matmul(pst1[:, hb:hb + 1], gw2b[:, kb, hb, :], gfh1[:, kb:kb + 1],
                               start=(kb == 0), stop=(kb == 3))
            return ins
        pl.stage("tensor", t_g2, waits=["g1c_3", "gmc_7"], inc="sT", tag=f"g2_{hb}")
    for hb in range(2):
        def s_g2c(s, hb=hb):
            return s.activation(out=gfh2[:, hb:hb + 1], in_=pst1[:, hb:hb + 1],
                                func=AF.Gelu, bias=gb2c[:, hb:hb + 1])
        pl.stage("scalar", s_g2c, waits=["g2_1"], inc="sS", tag=f"g2c_{hb}")

    def t_g3(t):
        t.matmul(pst0[0:1, 0:1], gw3b[:, 0:1], gfh2[:, 0:1], start=True, stop=False)
        return t.matmul(pst0[0:1, 0:1], gw3b[:, 1:2], gfh2[:, 1:2],
                        start=False, stop=True)
    pl.stage("tensor", t_g3, waits=["g2c_1", "g1c_3"], inc="sT", tag="g3")

    def s_g3(s):
        return s.activation(out=abcol[0:1, 2:3], in_=pst0[0:1, 0:1], func=AF.Tanh,
                            scale=consts[0:1, 4:5], bias=bias4[0:1, 1:2])
    pl.stage("scalar", s_g3, waits=["g3", "v_halfb4"], inc="sS", tag="g3act")

    def v_coef(v):
        i1 = v.scalar_tensor_tensor(out=abcol[0:1, 2:3], in0=abcol[0:1, 2:3], scalar=1.0,
                                    in1=abcol[0:1, 3:4], op0=OP.mult, op1=OP.mult)
        fence(v, i1)
        v.tensor_scalar(out=abcol[0:1, 1:2], in0=abcol[0:1, 2:3], scalar1=0.7,
                        scalar2=0.1, op0=OP.mult, op1=OP.add)
        return v.tensor_scalar(out=abcol[0:1, 0:1], in0=abcol[0:1, 3:4], scalar1=0.3,
                               scalar2=None, op0=OP.mult)
    pl.stage("vector", v_coef, waits=["g3act", "wl_plain"], inc="sV", tag="coef_row")

    def v_coefb(v):
        return v.tensor_copy(out=rowsb[0:1, 4, 0:2], in_=abcol[0:1, 0:2])
    pl.stage("vector", v_coefb, waits=["coef_row"], inc="sV", tag="coef_rowb")

    def t_coef_bc(t):
        return t.matmul(pst0[:, 0:2], ones_row_b[0:1, :], rowsb[0:1, 4, 0:2],
                        start=True, stop=True)
    pl.stage("tensor", t_coef_bc, waits=["coef_rowb", "g3"], inc="sT", tag="coef_bc")

    def v_coefc(v):
        return v.tensor_copy(out=abcol[:, 0:2], in_=pst0[:, 0:2])
    pl.stage("vector", v_coefc, waits=["coef_bc"], inc="sV", tag="coef")

    fobs = [fob, fln2]

    def b_ld(k):
        sl = k % 2

        @dma_n(2)
        def s_bld(s, k=k, sl=sl):
            i1 = s.dma_start(out=fobs[sl][:, :, :],
                             in_=ln2_sp.ap()[k].rearrange("(a p) c -> p a c", p=P))
            i2 = s.dma_start(out=tfc[:, k, :],
                             in_=tf0_sp.ap()[k].rearrange("(a p) -> p a", p=P),
                             allow_slow_non_contiguous=True)
            return [i1, i2]
        bw = [f"A{k}_ln2sp_3", f"A{k}_tfsp"]
        if k == 1:
            bw.append(f"A{NBLK-1}_ln2sp_3")
        if k >= 2:
            bw.append(f"B{k-2}_out")
        pl.stage("sync", s_bld, waits=bw, inc="dBL", dma=True, tag=f"B{k}_ld")

    CHN = [0]
    prev_chain = [None]

    def b_cycle(k, r):
        K = f"B{k}"
        sl = k % 2
        fo = fobs[sl]
        cw = [f"{K}_ld", "coef"] if r == 0 else [f"{K}_c{r}_tfc"]

        def v_cc(v, k=k):
            ic = None
            for tcol in range(4):
                ic = v.scalar_tensor_tensor(out=ccol[:, tcol:tcol + 1],
                                            in0=tfc[:, k, tcol:tcol + 1],
                                            scalar=abcol[:, 0:1],
                                            in1=abcol[:, 1:2],
                                            op0=OP.mult, op1=OP.add)
            fence(v, ic)
            return v.tensor_scalar(out=ncc[:, :], in0=ccol[:, :], scalar1=-1.0,
                                   scalar2=None, op0=OP.mult)
        pl.stage("vector", v_cc, waits=cw, inc="sV", tag=f"{K}_c{r}_cc")

        for ti in range(4):
            def v_tw(v, ti=ti, fo=fo):
                q = [fo[:, ti, 0:DQ], fo[:, ti, DQ:2 * DQ],
                     fo[:, ti, 2 * DQ:3 * DQ], fo[:, ti, 3 * DQ:D]]
                v.tensor_copy(out=ytmp[:, 0, :], in_=q[0])
                v.tensor_copy(out=ytmp[:, 1, :], in_=q[1])
                v.scalar_tensor_tensor(out=q[0], in0=q[2], scalar=ccol[:, ti:ti + 1],
                                       in1=q[0], op0=OP.mult, op1=OP.add)
                v.scalar_tensor_tensor(out=q[1], in0=q[3], scalar=ncc[:, ti:ti + 1],
                                       in1=q[1], op0=OP.mult, op1=OP.add)
                v.scalar_tensor_tensor(out=q[2], in0=ytmp[:, 0, :],
                                       scalar=ncc[:, ti:ti + 1], in1=q[2],
                                       op0=OP.mult, op1=OP.add)
                return v.scalar_tensor_tensor(out=q[3], in0=ytmp[:, 1, :],
                                              scalar=ccol[:, ti:ti + 1], in1=q[3],
                                              op0=OP.mult, op1=OP.add)
            pl.stage("vector", v_tw, waits=[f"{K}_c{r}_cc"], inc="sV",
                     tag=f"{K}_c{r}_tw_{ti}")

        if r >= 2:
            return None

        KC = f"{K}_c{r}"
        ci = CHN[0]
        CHN[0] += 1
        prev = prev_chain[0]

        def v_cast(v, fo=fo):
            ins = None
            for ti in range(4):
                ins = v.tensor_copy(out=fbf[:, ti, :], in_=fo[:, ti, :])
            return ins
        caw = [f"{K}_c{r}_tw_3"]
        if prev is None:
            caw += [f"A{NBLK-1}_fxbar", f"A{NBLK-1}_gf_3"]
        else:
            caw.append(f"{prev}_xbar")
        pl.stage("vector", v_cast, waits=caw, inc="sV", tag=f"{KC}_cast")

        xw = [f"{KC}_cast",
              f"{prev}_h1_mm_7" if prev else f"A{NBLK-1}_h1_mm_7"]

        def s_bxbar(s):
            return s.dma_start_transpose(
                out=ffm[:, :, :, :].rearrange("p a b c -> p (a b c)")
                .rearrange("p (a b) -> p a b", b=P),
                in_=fbf[:, :, :].rearrange("p a b -> p (a b)")
                .rearrange("p (a b) -> p a b", b=P))
        pl.stage("sync", s_bxbar, waits=xw, inc="dBX", dma=True, tag=f"{KC}_xbar")

        we = [f"{prev}_tfrow"] if prev else ["coef"]
        ffn_mms(KC, lambda kb: ffm[:, :, kb, :], w1b, 8, 8, "h1", b1c,
                pst0, pst1, f"{KC}_xbar", we)
        ffn_mms(KC, lambda kb: fh1[:, kb, :], w2b, 8, 4, "h2", b2c,
                pst0, pst1, f"{KC}_h1_ge_7", [])
        ffn_mms(KC, lambda kb: fh2[:, kb, :], w3b, 4, 2, "h3", b3c,
                pst0, pst1, f"{KC}_h2_ge_3", [])

        tmw = [f"{KC}_h3_ge_1"]
        if prev:
            tmw.append(f"{prev}_tfrow")

        def t_btf(t):
            t.matmul(pst0[0:1, :], w4b[:, 0:1], fh3[:, 0, :], start=True,
                     stop=False)
            return t.matmul(pst0[0:1, :], w4b[:, 1:2], fh3[:, 1, :],
                            start=False, stop=True)
        pl.stage("tensor", t_btf, waits=tmw, inc="sT", tag=f"{KC}_tfmm")

        def s_btf(s):
            return s.activation(out=frow[0:1, :], in_=pst0[0:1, :], func=AF.Tanh,
                                scale=consts[0:1, 4:5], bias=bias4[0:1, 0:1])
        pl.stage("scalar", s_btf, waits=[f"{KC}_tfmm"], inc="sS",
                 tag=f"{KC}_tfrow")

        pyt = py[ci % 2]
        trw = [f"{KC}_tfrow"]
        trw.append(CHAIN_TFC[ci - 2] if ci >= 2 else "gfev")

        def t_tftr(t, pyt=pyt):
            ins = None
            for ti in range(4):
                ins = t.transpose(pyt[:, ti:ti + 1],
                                  frow[0:1, ti * P:(ti + 1) * P],
                                  identf[0:1, 0:1])
            return ins
        pl.stage("tensor", t_tftr, waits=trw, inc="sT", tag=f"{KC}_tftr")
        prev_chain[0] = KC
        return (KC, k, r, pyt)

    CHAIN_TFC = {}

    def b_tfc(ch):
        KC, k, r, pyt = ch

        def v_tfc(v, k=k, pyt=pyt):
            return v.tensor_copy(out=tfc[:, k, :], in_=pyt[:, 0:4])
        pl.stage("vector", v_tfc, waits=[f"{KC}_tftr"], inc="sV",
                 tag=f"{KC}_tfc_3")
        pl.alias(f"B{k}_c{r+1}_tfc", f"{KC}_tfc_3")
        ci = CHN[0] - 1
        for idx, (kk, rr) in enumerate(CHAIN_IDS):
            if (kk, rr) == (k, r):
                CHAIN_TFC[idx] = f"{KC}_tfc_3"

    CHAIN_IDS = []

    def b_final(k):
        K = f"B{k}"
        sl = k % 2
        fo = fobs[sl]

        def s_fx2ld(s, k=k):
            return s.dma_start(out=fx2[:, :, :],
                               in_=x2_sp.ap()[k].rearrange("(a p) c -> p a c", p=P))
        fw = [f"A{k}_x2sp_3"]
        if k == 0:
            fw += [f"A{NBLK-1}_ln2c", f"A{NBLK-1}_x2sp_3"]
        else:
            fw.append(f"B{k-1}_p5_3")
        pl.stage("sync", s_fx2ld, waits=fw, inc="dX2", dma=True, tag=f"{K}_fx2ld")

        for ti in range(4):
            def v_p1(v, ti=ti, fo=fo):
                return v.reduce_sum(out=m2col[:, ti:ti + 1], in_=fo[:, ti, :],
                                    axis=AX.X)
            pl.stage("vector", v_p1, waits=[f"{K}_c2_tw_{ti}"], inc="sV",
                     tag=f"{K}_p1_{ti}")

            def s_p2(s, ti=ti, fo=fo):
                return s.activation(out=hsl[:, :], in_=fo[:, ti, :],
                                    func=AF.Square, accum_out=r2col[:, ti:ti + 1])
            pl.stage("scalar", s_p2, waits=[f"{K}_p1_{ti}"], inc="sS",
                     tag=f"{K}_p2_{ti}")

            def v_p3(v, ti=ti):
                v.tensor_scalar(out=m2col[:, ti:ti + 1], in0=m2col[:, ti:ti + 1],
                                scalar1=1.0 / D, scalar2=None, op0=OP.mult)
                i2r = v.tensor_scalar(out=r2col[:, ti:ti + 1], in0=r2col[:, ti:ti + 1],
                                      scalar1=1.0 / D, scalar2=None, op0=OP.mult)
                fence(v, i2r)
                return v.scalar_tensor_tensor(out=r2col[:, ti:ti + 1],
                                              in0=m2col[:, ti:ti + 1],
                                              scalar=m2col[:, ti:ti + 1],
                                              in1=r2col[:, ti:ti + 1],
                                              op0=OP.mult, op1=OP.subtract)
            pl.stage("vector", v_p3, waits=[f"{K}_p2_{ti}"], inc="sV",
                     tag=f"{K}_p3_{ti}")

            def s_p4(s, ti=ti):
                s.activation(out=r2col[:, ti:ti + 1], in_=r2col[:, ti:ti + 1],
                             func=AF.Ln, scale=consts[:, 2:3], bias=consts[:, 0:1])
                return s.activation(out=r2col[:, ti:ti + 1], in_=r2col[:, ti:ti + 1],
                                    func=AF.Exp, scale=consts[:, 3:4])
            pl.stage("scalar", s_p4, waits=[f"{K}_p3_{ti}"], inc="sS",
                     tag=f"{K}_p4_{ti}")

            def v_p5(v, ti=ti, fo=fo):
                i2 = v.scalar_tensor_tensor(out=nm2col[:, ti:ti + 1],
                                            in0=m2col[:, ti:ti + 1], scalar=-1.0,
                                            in1=r2col[:, ti:ti + 1],
                                            op0=OP.mult, op1=OP.mult)
                fence(v, i2)
                v.tensor_scalar(out=fo[:, ti, :], in0=fo[:, ti, :],
                                scalar1=r2col[:, ti:ti + 1], scalar2=nm2col[:, ti:ti + 1],
                                op0=OP.mult, op1=OP.add)
                return v.tensor_tensor(out=fo[:, ti, :], in0=fo[:, ti, :],
                                       in1=fx2[:, ti, :], op=OP.add)
            pl.stage("vector", v_p5, waits=[f"{K}_p4_{ti}", f"{K}_fx2ld"],
                     inc="sV", tag=f"{K}_p5_{ti}")

        def s_out(s, k=k, fo=fo):
            return s.dma_start(out=out_tok[k].rearrange("(a p) c -> p a c", p=P),
                               in_=fo[:, :, :])
        pl.stage("sync", s_out, waits=[f"{K}_p5_3"], inc="dOUT", dma=True,
                 tag=f"{K}_out")

    for p0 in range(0, NBLK, 2):
        pr = list(range(p0, min(p0 + 2, NBLK)))
        if len(pr) == 2:
            a, b = pr
            b_ld(a)
            b_ld(b)
            CHAIN_IDS += [(a, 0), (b, 0), (a, 1), (b, 1)]
            ch_a0 = b_cycle(a, 0)
            ch_b0 = b_cycle(b, 0)
            b_tfc(ch_a0)
            ch_a1 = b_cycle(a, 1)
            b_tfc(ch_b0)
            ch_b1 = b_cycle(b, 1)
            b_tfc(ch_a1)
            b_cycle(a, 2)
            b_final(a)
            b_tfc(ch_b1)
            b_cycle(b, 2)
            b_final(b)
        else:
            a = pr[0]
            b_ld(a)
            CHAIN_IDS += [(a, 0), (a, 1)]
            ch_a0 = b_cycle(a, 0)
            b_tfc(ch_a0)
            ch_a1 = b_cycle(a, 1)
            b_tfc(ch_a1)
            b_cycle(a, 2)
            b_final(a)

    final_out = pl.cnt["dOUT"]

    sems = {}
    for name in pl.cnt:
        sems[name] = ctx.enter_context(nc.semaphore(name))
    for e in ENGS:
        FENCE["sems"][e] = ctx.enter_context(nc.semaphore(f"fence_{e}"))
        FENCE["cnt"][e] = 0

    block = ctx.enter_context(nc.Block())

    def emit(eng_name, stream):
        FENCE["eng"] = eng_name
        for waits, fn, inc, dma, gidx, snap, seramt in pl.items[eng_name]:
            if max_stages is not None and gidx > max_stages:
                continue
            if serialize:
                for sn, sv in snap.items():
                    if sv > 0:
                        stream.wait_ge(sems[sn], sv)
            for w in waits:
                sname, val = pl.marks[w]
                stream.wait_ge(sems[sname], val)
            ret = fn(stream)
            if inc is not None:
                if dma:
                    insts = ret if isinstance(ret, list) else [ret]
                    for i in insts:
                        i.then_inc(sems[inc], 16)
                else:
                    ret.then_inc(sems[inc], 1)

    @block.sync
    def _(sync):
        emit("sync", sync)
        if max_stages is None:
            sync.wait_ge(sems["dOUT"], final_out)

    @block.vector
    def _(vector):
        emit("vector", vector)

    @block.scalar
    def _(scalar):
        emit("scalar", scalar)

    @block.tensor
    def _(tensor):
        emit("tensor", tensor)

    @block.gpsimd
    def _(gpsimd):
        emit("gpsimd", gpsimd)

    ctx.close()
    return nc


_NC_CACHE = {}


def _get_nc():
    if "nc" not in _NC_CACHE:
        _NC_CACHE["nc"] = build_program(4)
    return _NC_CACHE["nc"]


def kernel(L0_attn_u, L0_attn_A_log, L0_attn_W_x, L0_attn_W_dt, L0_attn_b_dt,
           L0_attn_D_param, L0_ffn_cn_w1, L0_ffn_cn_b1, L0_ffn_cn_w2, L0_ffn_cn_b2,
           L0_ffn_cn_w3, L0_ffn_cn_b3, L0_ffn_cn_w4, L0_ffn_cn_b4, L0_ffn_gc_w1,
           L0_ffn_gc_b1, L0_ffn_gc_w2, L0_ffn_gc_b2, L0_ffn_gc_w3, L0_ffn_gc_b3,
           L0_ffn_adaptive_range):
    f32 = np.float32
    u = np.ascontiguousarray(np.asarray(L0_attn_u, dtype=f32))
    A_log = np.asarray(L0_attn_A_log, f32)
    W_x = np.asarray(L0_attn_W_x, f32)
    W_dt = np.asarray(L0_attn_W_dt, f32)
    b_dt = np.asarray(L0_attn_b_dt, f32)
    D_param = np.asarray(L0_attn_D_param, f32)

    def c(a):
        return np.ascontiguousarray(a, dtype=f32)

    w1h = c(np.asarray(L0_ffn_cn_w1, f32).reshape(8, P, 8, P).transpose(1, 0, 2, 3))
    w2h = c(np.asarray(L0_ffn_cn_w2, f32).reshape(8, P, 4, P).transpose(1, 0, 2, 3))
    w3h = c(np.asarray(L0_ffn_cn_w3, f32).reshape(4, P, 2, P).transpose(1, 0, 2, 3))
    w4h = c(np.asarray(L0_ffn_cn_w4, f32).reshape(2, P).T)
    b1h = c(np.asarray(L0_ffn_cn_b1, f32).reshape(8, P).T)
    b2h = c(np.asarray(L0_ffn_cn_b2, f32).reshape(4, P).T)
    b3h = c(np.asarray(L0_ffn_cn_b3, f32).reshape(2, P).T)
    gw1h = c(np.asarray(L0_ffn_gc_w1, f32).reshape(8, P, 4, P).transpose(1, 0, 2, 3))
    gw2h = c(np.asarray(L0_ffn_gc_w2, f32).reshape(4, P, 2, P).transpose(1, 0, 2, 3))
    gw3h = c(np.asarray(L0_ffn_gc_w3, f32).reshape(2, P).T)
    gb1h = c(np.asarray(L0_ffn_gc_b1, f32).reshape(4, P).T)
    gb2h = c(np.asarray(L0_ffn_gc_b2, f32).reshape(2, P).T)
    b4h = c(np.asarray(L0_ffn_cn_b4, f32).reshape(1, 1))
    gb3h = c(np.asarray(L0_ffn_gc_b3, f32).reshape(1, 1))
    arh = c(np.asarray(L0_ffn_adaptive_range, f32).reshape(1, 1))
    wxth = c(W_x.T)
    identh = np.eye(P, dtype=f32)

    in_maps = []
    for core in range(8):
        b = core // 4
        qq = core % 4
        aqh = c(A_log[DQ * qq:DQ * (qq + 1)].reshape(2, P, NST)
                .transpose(1, 0, 2).reshape(P, 32))
        m0 = 1.0 if b == 0 else 0.0
        mh = np.empty((P, 2), f32)
        mh[:, 0] = m0
        mh[:, 1] = 1.0 - m0
        utokh = np.stack([u[b, 2048 * k + CH * qq: 2048 * k + CH * (qq + 1), :]
                          for k in range(NBLK)])
        in_maps.append(dict(
            u_b=u[b],
            u_q=c(u[b][:, DQ * qq:DQ * (qq + 1)]),
            u_tok=c(utokh),
            wxt=wxth,
            wdtq=c(W_dt[DQ * qq:DQ * (qq + 1), :].T),
            aq=aqh,
            bdtbc=c(np.broadcast_to(b_dt[DQ * qq:DQ * (qq + 1)], (P, DQ))),
            dbc=c(np.broadcast_to(D_param[DQ * qq:DQ * (qq + 1)], (P, DQ))),
            masks=mh,
            ident_in=identh,
            cn_w1=w1h, cn_w2=w2h, cn_w3=w3h, cn_w4=w4h,
            cn_b1=b1h, cn_b2=b2h, cn_b3=b3h, cn_b4=b4h,
            gc_w1=gw1h, gc_w2=gw2h, gc_w3=gw3h,
            gc_b1=gb1h, gc_b2=gb2h, gc_b3=gb3h,
            arr=arh,
        ))

    nc = _get_nc()
    res = run_bass_kernel_spmd(nc, in_maps, core_ids=list(range(8)))

    out = np.empty((2, L, D), dtype=f32)
    for core in range(8):
        b = core // 4
        qq = core % 4
        ot = np.asarray(res.results[core]["out_tok"])
        for k in range(NBLK):
            out[b, 2048 * k + CH * qq: 2048 * k + CH * (qq + 1), :] = ot[k]
    return out


if __name__ == "__main__":
    import time
    t0 = time.time()
    nc = build_program()
    n_inst = sum(1 for _ in nc.all_instructions())
    print(f"built in {time.time()-t0:.1f}s, {n_inst} instructions")
